# version 17
# baseline (speedup 1.0000x reference)
import numpy as np
import ml_dtypes
import concourse.bass as bass
import concourse.mybir as mybir
from concourse.bass_utils import run_bass_kernel_spmd

F32 = mybir.dt.float32
BF16 = mybir.dt.bfloat16
F16 = mybir.dt.float16
I8 = mybir.dt.int8
AF = mybir.ActivationFunctionType
ALU = mybir.AluOpType
AX = mybir.AxisListType.X
BF = ml_dtypes.bfloat16

D = 2048
NT = 16
TPC = NT * 128
NK = 16
ST = 2
NPAIR = 512
LN_EPS = 1e-5
Q_EPS = 1e-7

PERM = np.concatenate([np.arange(0, 1024, 2), np.arange(1, 1024, 2),
                       np.arange(1024, 2048)]).astype(np.int64)

ENGINES = ("sync", "scalar", "vector", "tensor", "gpsimd")


class _Sem:
    __slots__ = ("name", "n", "h")

    def __init__(self, name):
        self.name = name
        self.n = 0
        self.h = None


class _Op:
    __slots__ = ("fn", "waits", "inc", "self_wait")

    def __init__(self, fn, waits, inc, self_wait):
        self.fn = fn
        self.waits = waits
        self.inc = inc
        self.self_wait = self_wait


class _Plan:
    def __init__(self):
        self.ops = {e: [] for e in ENGINES}
        self.sems = []
        self.auto = {}

    def sem(self, name):
        s = _Sem(name)
        self.sems.append(s)
        return s

    def add(self, eng, fn, waits=(), inc=None):
        ws = [w for w in waits if w is not None]
        self_wait = None
        if eng in self.auto:
            a = self.auto[eng]
            if inc is None:
                inc = (a, 1)
            if a.n > 0:
                self_wait = (a, a.n)
        self.ops[eng].append(_Op(fn, ws, inc, self_wait))
        if inc is not None:
            inc[0].n += inc[1]

    def emit(self, block):
        last = {}

        def mk(ename):
            def run(eng):
                for op in self.ops[ename]:
                    sw = [op.self_wait] if op.self_wait is not None else []
                    for (s, v) in list(op.waits) + sw:
                        if v <= 0:
                            continue
                        key = (ename, id(s))
                        if last.get(key, 0) < v:
                            eng.wait_ge(s.h, v)
                            last[key] = v
                    inst = op.fn(eng)
                    if op.inc is not None:
                        inst.then_inc(op.inc[0].h, op.inc[1])
            return run

        block.sync(mk("sync"))
        block.scalar(mk("scalar"))
        block.vector(mk("vector"))
        block.tensor(mk("tensor"))
        block.gpsimd(mk("gpsimd"))


def build_program(limit=99, limit2=99, p2_tiles=NT, dbg=False):
    nc = bass.Bass("TRN2", target_bir_lowering=False, debug=False)

    x0d = nc.dram_tensor("x0", [TPC, D], F32, kind="ExternalInput").ap()
    wqd = nc.dram_tensor("wq", [16, 128, D], F32, kind="ExternalInput").ap()
    wkvd = nc.dram_tensor("wkv", [8, 128, 8192], F32, kind="ExternalInput").ap()
    woutd = nc.dram_tensor("wout", [128, NK * D], F32, kind="ExternalInput").ap()
    wfd = nc.dram_tensor("wf", [128, NK * D], I8, kind="ExternalInput").ap()
    wsd = nc.dram_tensor("ws", [1, D], F32, kind="ExternalInput").ap()
    bqd = nc.dram_tensor("bq", [D], F32, kind="ExternalInput").ap()
    bkd = nc.dram_tensor("bk", [1, D], F32, kind="ExternalInput").ap()
    bvd = nc.dram_tensor("bv", [1, D], F32, kind="ExternalInput").ap()
    bod = nc.dram_tensor("bo", [1, D], F32, kind="ExternalInput").ap()
    fbd = nc.dram_tensor("fb", [1, D], F32, kind="ExternalInput").ap()
    ttokd = nc.dram_tensor("ttok", [TPC, 2 * NPAIR], BF16, kind="ExternalInput").ap()
    tfeatd = nc.dram_tensor("tfeat", [NPAIR, 2 * TPC], BF16, kind="ExternalInput").ap()
    identd = nc.dram_tensor("ident", [128, 128], BF16, kind="ExternalInput").ap()
    ones32d = nc.dram_tensor("ones32", [1, 128], F32, kind="ExternalInput").ap()
    outd = nc.dram_tensor("out", [TPC, D], F32, kind="ExternalOutput").ap()

    ik = "ExternalOutput" if dbg else "Internal"
    x1sp = nc.dram_tensor("x1sp", [TPC, D], BF16, kind=ik).ap()
    wfsp = nc.dram_tensor("wfsp", [128, NK * D], BF16).ap()
    ytsp = nc.dram_tensor("ytsp", [NT, 128, NK * 128], BF16).ap()
    arin = nc.dram_tensor("arin", [128, 16], F32).ap()
    arout = nc.dram_tensor("arout", [128, 16], F32).ap()

    p = _Plan()
    SV = p.sem("sv")
    SS = p.sem("ss")
    SP = p.sem("sp")
    CC = p.sem("cc")
    p.auto = {"vector": SV, "scalar": SS}
    sdc = p.sem("sdc")
    swsp = p.sem("swsp")
    sar = p.sem("sar")
    sspy = [p.sem("sspy0"), p.sem("sspy1"), p.sem("sspy2")]
    sgb = p.sem("sgb")
    sgwf = p.sem("sgwf")
    sgwo = p.sem("sgwo")
    sx0 = [p.sem("sx0a"), p.sem("sx0b"), p.sem("sx0c"),
           p.sem("sx0d"), p.sem("sx0e")]
    sx1f = p.sem("sx1f")
    sx1s = [p.sem("sx1sa"), p.sem("sx1sb")]
    syt = [p.sem("syta"), p.sem("sytb")]
    sx1r = [p.sem("sx1ra"), p.sem("sx1rb")]
    sout = [p.sem("souta"), p.sem("soutb")]
    sgwkv = [p.sem("sgwkva"), p.sem("sgwkvb")]
    sgwq = [p.sem("sgwqa"), p.sem("sgwqb")]
    sgt = [p.sem("sgta"), p.sem("sgtb")]

    def V(): return (SV, SV.n)
    def S(): return (SS, SS.n)
    def P(): return (SP, SP.n)
    def T(s): return (s, s.n)

    guards = []

    def sbuf(name, shape, dt):
        g = nc.sbuf_tensor(name, shape, dt)
        guards.append(g)
        return g.__enter__()

    def psum(name, shape, dt):
        g = nc.psum_tensor(name, shape, dt)
        guards.append(g)
        return g.__enter__()

    A1 = sbuf("A1", [128, NK * D], BF16)
    A2 = sbuf("A2", [128, NK * D], BF16)
    WKV = sbuf("WKV", [128, 8 * D], BF16)
    X32 = sbuf("X32", [128, 2 * D], F32)
    TBL = sbuf("TBL", [128, 2048], BF16)
    FF16 = sbuf("FF16", [128, 1024], F16)
    QI8 = sbuf("QI8", [128, D], I8)
    BNS = sbuf("BNS", [128, 32], F32)
    STAT = sbuf("STAT", [128, 80], F32)
    STATB = sbuf("STATB", [128, 48], BF16)
    ROWS = sbuf("ROWS", [128, 4352], BF16)
    BQ = sbuf("BQ", [128, 16], F32)
    ASB = sbuf("ASB", [128, 16], F32)
    IDENT = sbuf("IDENT", [128, 128], BF16)
    IDENTF = sbuf("IDENTF", [128, 128], F32)
    SCRF = X32[:, 3072:4096]
    X0B = sbuf("X0B", [128, 3 * D], BF16)

    PT = [psum("PT0", [128, 1024], BF16), psum("PT1", [128, 1024], BF16)]
    PF = [psum("PF%d" % i, [128, 512], F32) for i in range(4)]
    PA = psum("PA", [128, 512], F32)
    PTF = psum("PTF", [128, 512], F32)

    def R_H(m):     return ROWS[0:1, (m % 4) * 128:(m % 4) * 128 + 128]
    def R_H2(m):    return ROWS[0:1, 512 + (m % 2) * 128:512 + (m % 2) * 128 + 128]
    def R_NM2(m):   return ROWS[0:1, 768 + (m % 2) * 128:768 + (m % 2) * 128 + 128]
    R_ONES512 = ROWS[0:1, 1024:1536]
    R_FB = ROWS[0:1, 1536:3584]
    R_ONES128A = ROWS[32:33, 0:128]
    R_BK = ROWS[32:33, 128:2176]
    R_BO = ROWS[32:33, 2176:4224]
    R_ONES128B = ROWS[64:65, 0:128]
    R_ZERO16 = ROWS[32:33, 4224:4240]
    R_BV = ROWS[64:65, 128:2176]
    X_WS = X32[0:1, 0:D]
    X_ONES32 = X32[0:1, D:D + 128]

    C_M1 = 0
    C_G = 2
    C_ISH = 4
    C_R2P1 = 8
    C_TA, C_TB, C_RSTD = 24, 25, 26
    C_M3 = 28
    C_G3 = 30
    C_ISH3 = 32
    C_SS = 34
    C_NM2 = 40
    C_R2 = 56
    C_SX1 = 35
    C_SX2 = 36
    C_MSQ = 37
    CB_NM2R2 = 0
    CB_RSTDK = 16
    CB_H = 32
    CB_H2 = 36

    p.add("sync", lambda e: e.dma_start(out=IDENT[:], in_=identd), inc=(sdc, 16))
    p.add("gpsimd", lambda e: e.dma_start(out=IDENTF[:], in_=identd),
          inc=(sgb, 16))
    p.add("sync", lambda e: e.dma_start(out=X_WS, in_=wsd), inc=(sdc, 16))
    p.add("sync", lambda e: e.dma_start(out=X_ONES32, in_=ones32d), inc=(sdc, 16))
    p.add("sync", lambda e: e.dma_start(out=BQ[:], in_=bqd.rearrange("(c p) -> p c", p=128),
                                        allow_slow_non_contiguous=True),
          inc=(sdc, 16))
    consts_d = T(sdc)
    p.add("vector", lambda e: e.memset(R_ZERO16, 0.0), inc=(SV, 1))
    zero16_v = V()

    p.add("gpsimd", lambda e: e.dma_start(out=R_BK, in_=bkd), inc=(sgb, 16))
    p.add("gpsimd", lambda e: e.dma_start(out=R_BV, in_=bvd), inc=(sgb, 16))
    p.add("gpsimd", lambda e: e.dma_start(out=R_BO, in_=bod), inc=(sgb, 16))
    p.add("gpsimd", lambda e: e.dma_start(out=R_FB, in_=fbd), inc=(sgb, 16))
    p.add("gpsimd", lambda e: e.dma_start(out=R_ONES128A, in_=ones32d), inc=(sgb, 16))
    p.add("gpsimd", lambda e: e.dma_start(out=R_ONES128B, in_=ones32d), inc=(sgb, 16))
    for q4 in range(4):
        p.add("gpsimd", (lambda e, q4=q4: e.dma_start(
            out=ROWS[0:1, 1024 + q4 * 128:1024 + (q4 + 1) * 128], in_=ones32d)),
            inc=(sgb, 16))
    bias_g = T(sgb)
    p.add("gpsimd", lambda e: e.dma_start(out=A2[:, :], in_=wfd), inc=(sgwf, 16))
    wf_g = T(sgwf)

    ws_copy_v = {0: None, 1: None}
    for n in range(4):
        bank = PF[n % 2]
        p.add("tensor", (lambda e, b=bank, n=n: e.matmul(
            b[:, :], X_ONES32, X32[0:1, n * 512:(n + 1) * 512],
            start=True, stop=True)), waits=[consts_d, ws_copy_v[n % 2]],
            inc=(SP, 1))
        p.add("vector", (lambda e, b=bank, n=n: e.tensor_copy(
            X32[:, n * 512:(n + 1) * 512], b[:, :])), waits=[P()], inc=(SV, 1))
        ws_copy_v[n % 2] = V()
    for k in range(NK):
        p.add("vector", (lambda e, k=k: e.tensor_tensor(
            A2[:, k * D:(k + 1) * D], A2[:, k * D:(k + 1) * D], X32[:, 0:D], ALU.mult)),
            waits=[wf_g], inc=(SV, 1))
    fold_v = V()
    p.add("sync", lambda e: e.dma_start(out=wfsp, in_=A2[:, :]),
          waits=[fold_v], inc=(swsp, 16))
    wfsp_d = T(swsp)

    def _finish():
        p.add("sync", lambda e: e.nop(), waits=[V(), S(), P()])
        sem_guards = []
        for s in p.sems:
            g = nc.semaphore(s.name)
            sem_guards.append(g)
            s.h = g.__enter__()
        blk = nc.Block()
        block = blk.__enter__()
        p.emit(block)
        blk.__exit__(None, None, None)
        for g in reversed(sem_guards + guards):
            g.__exit__(None, None, None)
        return nc

    if limit <= 0:
        return _finish()

    x0_free = [None] * (NT + 4)
    x1_spill_d = [None] * NT
    nm2_v = [None] * NT
    q1t_read_p = [None] * (NT // ST + 4)
    tmpb_free = [None] * (NT + 4)
    ln2t_read_p = [None] * (NT + 4)
    pt_free = {}
    pf_free = {}
    ff16_free = {0: None, 1: None}
    qtbl_free = {0: None, 1: None}
    qbf_done_s = {0: None}
    q1t_evac = {}
    hrow_done = [None] * NT

    def pt_w(bank):
        return pt_free.get(id(bank))

    def pf_w(bank):
        return pf_free.get(id(bank))

    QB0 = 8192

    def x0ap(m, c0=0, c1=D):
        s = m % 5
        if s < 3:
            return X0B[:, s * D + c0:s * D + c1]
        if s == 3:
            return TBL[:, c0:c1]
        return X32[:, c0:c1]

    def ffn1_partA(st):
        stb = st % 2
        for m2 in range(ST):
            mt = st * ST + m2
            mpar = mt % 2
            for n in range(4):
                bank = PF[2 * (n % 2) + m2]
                for k in range(NK):
                    p.add("tensor", (lambda e, bank=bank, k=k, n=n, stb=stb, m2=m2:
                          e.matmul(bank[:, :],
                                   WKV[:, stb * 4096 + m2 * 2048 + k * 128:
                                       stb * 4096 + m2 * 2048 + (k + 1) * 128],
                                   A2[:, k * D + n * 512:k * D + (n + 1) * 512],
                                   start=(k == 0), stop=False)),
                        waits=[q1t_evac[mt][k]] +
                        ([fold_v, pf_w(bank), hrow_done[st * ST + ST - 1],
                          bias_g] if k == 0 else []),
                        inc=None)
                p.add("tensor", (lambda e, bank=bank, n=n, mt=mt: e.matmul(
                    bank[:, :], R_H(mt), R_FB[0:1, n * 512:(n + 1) * 512],
                    start=False, stop=True)),
                    inc=(SP, 1))
                mm_p = P()
                sl = n % 2
                mi4 = mt % 4
                p.add("scalar", (lambda e, bank=bank, sl=sl, mi4=mi4: e.activation(
                    FF16[:, sl * 512:(sl + 1) * 512], bank[:, :], AF.Identity,
                    bias=0.0, scale=STAT[:, C_ISH + mi4:C_ISH + mi4 + 1])),
                    waits=[mm_p, ff16_free[sl]], inc=(SS, 1))
                pf_free[id(bank)] = S()
                ffs = S()
                p.add("vector", (lambda e, sl=sl, mt=mt, n=n:
                      e.tensor_tensor(
                          x0ap(mt, n * 512, (n + 1) * 512),
                          x0ap(mt, n * 512, (n + 1) * 512),
                          FF16[:, sl * 512:(sl + 1) * 512], ALU.add)),
                    waits=[ffs], inc=(SV, 1))
                ff16_free[sl] = V()
            x1_v = V()
            sp_sem = sx1f if mt % 5 == 4 else sx1s[mpar]
            p.add("gpsimd" if mt % 5 == 4 else "sync",
                  (lambda e, mt=mt: e.dma_start(
                      out=x1sp[mt * 128:(mt + 1) * 128, :],
                      in_=x0ap(mt))),
                  waits=[x1_v], inc=(sp_sem, 16))
            x1_spill_d[mt] = T(sp_sem)
            for c in range(4):
                p.add("vector", (lambda e, c=c, mt=mt: e.bn_stats(
                    BNS[:, c * 6:(c + 1) * 6],
                    x0ap(mt, c * 512, (c + 1) * 512))), inc=None)
            p.add("vector", lambda e: e.bn_aggr(BNS[:, 24:26], BNS[:, 0:24]),
                  inc=None)
            p.add("vector", lambda e: e.tensor_scalar(
                STAT[:, C_RSTD:C_RSTD + 1], BNS[:, 25:26], LN_EPS, None, ALU.add),
                inc=(SV, 1))
            ln2st_v = V()
            p.add("scalar", lambda e: e.activation(
                STAT[:, C_RSTD:C_RSTD + 1], STAT[:, C_RSTD:C_RSTD + 1], AF.Ln,
                bias=0.0, scale=1.0), waits=[ln2st_v], inc=None)
            p.add("scalar", lambda e: e.activation(
                STAT[:, C_RSTD:C_RSTD + 1], STAT[:, C_RSTD:C_RSTD + 1], AF.Exp,
                bias=0.0, scale=-0.5), inc=(SS, 1))
            r2_s = S()
            p.add("vector", (lambda e, mt=mt: e.tensor_scalar(
                STAT[:, C_R2P1 + mt:C_R2P1 + mt + 1], STAT[:, C_RSTD:C_RSTD + 1],
                1.0, None, ALU.add)), waits=[r2_s], inc=None)
            p.add("vector", (lambda e, mt=mt: e.tensor_copy(
                STAT[:, C_R2 + mt:C_R2 + mt + 1], STAT[:, C_RSTD:C_RSTD + 1])),
                inc=None)
            p.add("vector", (lambda e, mt=mt: e.scalar_tensor_tensor(
                STAT[:, C_NM2 + mt:C_NM2 + mt + 1], BNS[:, 24:25], -1.0,
                STAT[:, C_RSTD:C_RSTD + 1], ALU.mult, ALU.mult)), inc=(SV, 1))
            nm2_v[mt] = V()
        q1t_read_p[st] = P()

    def ln2_partB(mt):
        p.add("scalar", (lambda e, mt=mt: e.activation(
            x0ap(mt), x0ap(mt),
            AF.Identity, bias=STAT[:, C_NM2 + mt:C_NM2 + mt + 1],
            scale=STAT[:, C_R2 + mt:C_R2 + mt + 1])),
            waits=[x1_spill_d[mt], nm2_v[mt]], inc=(SS, 1))
        ln2_s = S()
        if mt % 5 == 4:
            for b in range(4):
                for j in range(4):
                    k = b * 4 + j
                    p.add("tensor", (lambda e, j=j, k=k, mt=mt:
                          e.transpose(
                              PTF[:, j * 128:(j + 1) * 128],
                              x0ap(mt, k * 128, (k + 1) * 128),
                              IDENTF[:])),
                        waits=[pt_w(PTF), ln2_s, bias_g] if j == 0 else [],
                        inc=(SP, 1) if j == 3 else None)
                tr_p = P()
                p.add("scalar", (lambda e, b=b, mt=mt: e.copy(
                    A1[:, b * 4 * D:(b * 4 + 4) * D].rearrange(
                        "p (k c) -> p k c", c=D)[:, :, mt * 128:(mt + 1) * 128],
                    PTF[:, 0:512].rearrange("p (k c) -> p k c", c=128))),
                    waits=[tr_p], inc=(SS, 1))
                pt_free[id(PTF)] = S()
        else:
            for half in range(2):
                ptb = PT[half]
                for j in range(8):
                    k = half * 8 + j
                    p.add("tensor", (lambda e, ptb=ptb, j=j, k=k, mt=mt:
                          e.transpose(
                              ptb[:, j * 128:(j + 1) * 128],
                              x0ap(mt, k * 128, (k + 1) * 128),
                              IDENT[:])),
                        waits=[pt_w(ptb), ln2_s] if j == 0 else [],
                        inc=(SP, 1) if j == 7 else None)
                tr_p = P()
                p.add("scalar", (lambda e, ptb=ptb, half=half, mt=mt: e.copy(
                    A1[:, half * 8 * D:(half * 8 + 8) * D].rearrange(
                        "p (k c) -> p k c", c=D)[:, :, mt * 128:(mt + 1) * 128],
                    ptb[:, 0:1024].rearrange("p (k c) -> p k c", c=128))),
                    waits=[tr_p], inc=(SS, 1))
                pt_free[id(ptb)] = S()
        ln2t_read_p[mt] = P()
        x0_free[mt] = ln2t_read_p[mt]

    for m in range(NT):
        par = m % 2
        st = m // ST
        stb = st % 2
        mm = m % ST
        i4 = m % 4
        if m >= 4:
            ln2_partB(m - 4)

        if m % 5 == 4:
            p.add("sync", (lambda e, m=m: e.dma_start(
                out=x0ap(m), in_=x0d[m * 128:(m + 1) * 128, :])),
                waits=[x0_free[m - 5] if m >= 5 else fold_v],
                inc=(sx0[m % 5], 16))
        else:
            p.add("gpsimd", (lambda e, m=m: e.dma_start(
                out=x0ap(m), in_=x0d[m * 128:(m + 1) * 128, :])),
                waits=[x0_free[m - 5] if m >= 5 else None],
                inc=(sx0[m % 5], 16))
        x0_d = T(sx0[m % 5])

        p.add("scalar", (lambda e, m=m, par=par: e.activation(
            WKV[:, QB0 + par * 2048:QB0 + (par + 1) * 2048], x0ap(m), AF.Square,
            bias=0.0, scale=1.0, accum_out=STAT[:, C_SX2:C_SX2 + 1])),
            waits=[x0_d, qtbl_free[par]], inc=None)
        p.add("scalar", (lambda e, m=m, par=par: e.activation(
            WKV[:, QB0 + par * 2048:QB0 + (par + 1) * 2048], x0ap(m), AF.Identity,
            bias=0.0, scale=1.0, accum_out=STAT[:, C_SX1:C_SX1 + 1])),
            inc=(SS, 1))
        sx_s = S()
        p.add("vector", (lambda e, m=m: e.tensor_reduce(
            STAT[:, C_TA:C_TA + 1], x0ap(m), AX, ALU.max)),
            waits=[x0_d], inc=None)
        p.add("vector", (lambda e, m=m: e.tensor_reduce(
            STAT[:, C_TB:C_TB + 1], x0ap(m), AX, ALU.min)), inc=None)
        p.add("vector", lambda e: e.tensor_scalar(
            BNS[:, 24:25], STAT[:, C_SX1:C_SX1 + 1], 1.0 / D, None, ALU.mult),
            waits=[sx_s], inc=None)
        p.add("vector", lambda e: e.tensor_tensor(
            STAT[:, C_MSQ:C_MSQ + 1], BNS[:, 24:25], BNS[:, 24:25], ALU.mult),
            inc=None)
        p.add("vector", lambda e: e.tensor_scalar(
            STAT[:, C_RSTD:C_RSTD + 1], STAT[:, C_SX2:C_SX2 + 1], 1.0 / D,
            STAT[:, C_MSQ:C_MSQ + 1], ALU.mult, ALU.subtract), inc=None)
        p.add("vector", lambda e: e.tensor_scalar(
            STAT[:, C_RSTD:C_RSTD + 1], STAT[:, C_RSTD:C_RSTD + 1], LN_EPS, None,
            ALU.add), inc=(SV, 1))
        stats_v = V()
        p.add("scalar", lambda e: e.activation(
            STAT[:, C_RSTD:C_RSTD + 1], STAT[:, C_RSTD:C_RSTD + 1], AF.Ln,
            bias=0.0, scale=1.0), waits=[stats_v], inc=None)
        p.add("scalar", lambda e: e.activation(
            STAT[:, C_RSTD:C_RSTD + 1], STAT[:, C_RSTD:C_RSTD + 1], AF.Exp,
            bias=0.0, scale=-0.5), inc=(SS, 1))
        rstd1_s = S()
        p.add("vector", (lambda e, par=par: e.tensor_copy(
            STAT[:, C_M1 + par:C_M1 + par + 1], BNS[:, 24:25])),
            waits=[rstd1_s], inc=None)
        p.add("vector", lambda e: e.tensor_tensor(
            STAT[:, C_TA:C_TA + 1], STAT[:, C_TA:C_TA + 1], BNS[:, 24:25], ALU.subtract),
            inc=None)
        p.add("vector", lambda e: e.tensor_tensor(
            STAT[:, C_TB:C_TB + 1], BNS[:, 24:25], STAT[:, C_TB:C_TB + 1],
            ALU.subtract), inc=None)
        p.add("vector", lambda e: e.tensor_tensor(
            STAT[:, C_TA:C_TA + 1], STAT[:, C_TA:C_TA + 1], STAT[:, C_TB:C_TB + 1],
            ALU.max), inc=None)
        p.add("vector", lambda e: e.tensor_tensor(
            STAT[:, C_TA:C_TA + 1], STAT[:, C_TA:C_TA + 1], STAT[:, C_RSTD:C_RSTD + 1],
            ALU.mult), inc=None)
        p.add("vector", lambda e: e.tensor_scalar(
            STAT[:, C_TA:C_TA + 1], STAT[:, C_TA:C_TA + 1], Q_EPS, None, ALU.max),
            inc=None)
        p.add("vector", (lambda e, i4=i4: e.tensor_scalar(
            STAT[:, C_ISH + i4:C_ISH + i4 + 1], STAT[:, C_TA:C_TA + 1],
            0.5 / 127.0, None, ALU.mult)), inc=None)
        p.add("vector", lambda e: e.reciprocal(
            STAT[:, C_TB:C_TB + 1], STAT[:, C_TA:C_TA + 1]), inc=None)
        p.add("vector", lambda e: e.tensor_scalar(
            STAT[:, C_TB:C_TB + 1], STAT[:, C_TB:C_TB + 1], 127.0, None, ALU.mult),
            inc=None)
        p.add("vector", (lambda e, par=par: e.tensor_tensor(
            STAT[:, C_G + par:C_G + par + 1], STAT[:, C_TB:C_TB + 1],
            STAT[:, C_RSTD:C_RSTD + 1], ALU.mult)), inc=None)
        p.add("vector", (lambda e, i4=i4: e.tensor_copy(
            STATB[:, CB_H + i4:CB_H + i4 + 1], STAT[:, C_TB:C_TB + 1])), inc=(SV, 1))
        hcol_v = V()

        p.add("vector", (lambda e, m=m, par=par: e.tensor_scalar(
            QI8[:], x0ap(m), STAT[:, C_M1 + par:C_M1 + par + 1],
            STAT[:, C_G + par:C_G + par + 1], ALU.subtract, ALU.mult)),
            waits=[qbf_done_s[0]], inc=(SV, 1))
        quant_v = V()
        p.add("scalar", (lambda e, par=par: e.copy(
            WKV[:, QB0 + par * 2048:QB0 + (par + 1) * 2048], QI8[:])),
            waits=[quant_v, qtbl_free[par]], inc=(SS, 1))
        qbf_done_s[0] = S()
        qbf_v = S()

        ptb = PT[m % 2]
        p.add("tensor", (lambda e, ptb=ptb, i4=i4: e.transpose(
            ptb[0:1, 0:128], STATB[:, CB_H + i4:CB_H + i4 + 1], IDENT[:])),
            waits=[pt_w(ptb), hcol_v, consts_d], inc=(SP, 1))
        p.add("scalar", (lambda e, ptb=ptb, m=m: e.copy(R_H(m), ptb[0:1, 0:128])),
              waits=[P()], inc=(SS, 1))
        pt_free[id(ptb)] = S()
        hrow_done[m] = S()

        for half in range(2):
            ptb = PT[half]
            for j in range(8):
                k = half * 8 + j
                p.add("tensor", (lambda e, ptb=ptb, j=j, k=k, par=par: e.transpose(
                    ptb[:, j * 128:(j + 1) * 128],
                    WKV[:, QB0 + par * 2048 + k * 128:QB0 + par * 2048 + (k + 1) * 128],
                    IDENT[:])),
                    waits=[pt_w(ptb), qbf_v] if j == 0 else [],
                    inc=(SP, 1) if j == 7 else None)
            tr_p = P()
            p.add("scalar", (lambda e, ptb=ptb, half=half, stb=stb, mm=mm: e.copy(
                WKV[:, stb * 4096 + mm * 2048 + half * 1024:
                    stb * 4096 + mm * 2048 + (half + 1) * 1024],
                ptb[:, 0:1024])),
                waits=[tr_p, q1t_read_p[st - 2] if st >= 2 else None],
                inc=(SS, 1))
            ev_s = S()
            for j in range(8):
                q1t_evac.setdefault(m, [None] * NK)[half * 8 + j] = ev_s
            pt_free[id(ptb)] = ev_s
        qtbl_free[par] = P()
        q1t_s = S()

        if mm == 1 and st >= 1:
            ffn1_partA(st - 1)

    ln2_partB(NT - 4)
    ln2_partB(NT - 3)
    ffn1_partA(NT // ST - 1)
    ln2_partB(NT - 2)
    ln2_partB(NT - 1)

    lnt_done_s = S()
    allffn1_p = q1t_read_p[NT // ST - 1]
    if limit <= 1:
        return _finish()

    wkv_buf_free = {0: None, 1: None}
    ptmp_free = {0: None, 1: None}
    rstdk_v = {}
    tbl_free = {0: ln2t_read_p[NT - 1], 1: ln2t_read_p[NT - 1]}
    kevac1_s = {}
    kevac3_s = {}
    rope_done_v = {}

    for n in range(8):
        b = (n + 1) % 2
        kv_load_q = []
        for qq in range(4):
            p.add("gpsimd", (lambda e, n=n, b=b, qq=qq: e.dma_start(
                out=WKV[:, b * 8192 + qq * 2048:b * 8192 + (qq + 1) * 2048],
                in_=wkvd[n][:, qq * 2048:(qq + 1) * 2048])),
                waits=([allffn1_p, wkv_buf_free[b]] if qq == 0
                       else [T(sgwkv[b])]),
                inc=(sgwkv[b], 16))
            kv_load_q.append(T(sgwkv[b]))
        kv_load = kv_load_q[3]

        is_v = n >= 4
        if n == 4:
            p.add("tensor", lambda e: e.matmul(
                PA[:, 0:16], R_ONES128A, R_ZERO16,
                start=True, stop=False, skip_group_check=True),
                waits=[zero16_v, bias_g], inc=None)
        for mg in range(4):
            banks_mm = []
            for mi in range(4):
                mtile = mg * 4 + mi
                bank = PF[mi]
                for k in range(NK):
                    p.add("tensor", (lambda e, bank=bank, k=k, b=b, mtile=mtile:
                          e.matmul(bank[:, :],
                                   A1[:, k * D + mtile * 128:k * D + (mtile + 1) * 128],
                                   WKV[:, b * 8192 + k * 512:b * 8192 + (k + 1) * 512],
                                   start=(k == 0), stop=False)),
                        waits=[kv_load_q[k // 4]] +
                        ([lnt_done_s, pf_w(bank)] if k == 0 else []), inc=None)
                brow = (ROWS[64:65, 128 + (n % 4) * 512:128 + (n % 4) * 512 + 512]
                        if is_v else
                        ROWS[32:33, 128 + (n % 4) * 512:128 + (n % 4) * 512 + 512])
                orow = R_ONES128B if is_v else R_ONES128A
                p.add("tensor", (lambda e, bank=bank, brow=brow, orow=orow: e.matmul(
                    bank[:, :], orow, brow,
                    start=False, stop=True)),
                    waits=[bias_g], inc=(SP, 1))
                banks_mm.append((bank, mtile, P()))
            for bank, mtile, mm_p in banks_mm:
                if not is_v:
                    p.add("scalar", (lambda e, bank=bank, mtile=mtile, n=n: e.copy(
                        A2[:, mtile * D + n * 512:mtile * D + (n + 1) * 512],
                        bank[:, :])), waits=[mm_p, wfsp_d], inc=(SS, 1))
                    pf_free[id(bank)] = S()
                    if n == 1:
                        kevac1_s[mtile] = S()
                    if n == 3:
                        kevac3_s[mtile] = S()
                else:
                    pi = mtile % 2
                    p.add("vector", (lambda e, bank=bank, mtile=mtile, n=n, pi=pi:
                          e.tensor_tensor(
                              TBL[:, pi * 512:(pi + 1) * 512], bank[:, :],
                              A2[:, mtile * D + (n - 4) * 512:
                                  mtile * D + (n - 4) * 512 + 512], ALU.mult)),
                        waits=[mm_p, ptmp_free[pi]], inc=(SV, 1))
                    pf_free[id(bank)] = V()
                    pt_v = V()
                    for c in range(4):
                        cc = 4 * (n - 4) + c
                        p.add("tensor", (lambda e, c=c, cc=cc, mtile=mtile, pi=pi:
                              e.matmul(PA[:, cc:cc + 1],
                                       TBL[:, pi * 512 + c * 128:
                                           pi * 512 + (c + 1) * 128],
                                       STATB[:, CB_RSTDK + mtile:CB_RSTDK + mtile + 1],
                                       start=False, stop=(mtile == 15),
                                       skip_group_check=True)),
                            waits=[pt_v, rstdk_v[mtile]] if c == 0 else [],
                            inc=(SP, 1) if c == 3 else None)
                    ptmp_free[pi] = P()
        wkv_buf_free[b] = P()

        if n in (1, 2):
            for mtile in (range(8) if n == 1 else range(8, 16)):
                r = mtile % 2
                tb_c = r * 1024
                tb_s = r * 1024 + 512
                p.add("sync", (lambda e, mtile=mtile, tb_c=tb_c: e.dma_start(
                    out=TBL[:, tb_c:tb_c + 1024],
                    in_=ttokd[mtile * 128:(mtile + 1) * 128, :])),
                    waits=[tbl_free[r]], inc=(sgt[r], 16))
                tbl_d = T(sgt[r])
                E = A2[:, mtile * D:mtile * D + 512]
                O = A2[:, mtile * D + 512:mtile * D + 1024]
                p.add("vector", (lambda e, E=E, tb_c=tb_c: e.tensor_tensor(
                    SCRF[:, 0:512], E, TBL[:, tb_c:tb_c + 512], ALU.mult)),
                    waits=[tbl_d, kevac1_s[mtile]], inc=None)
                p.add("vector", (lambda e, E=E, tb_s=tb_s: e.tensor_tensor(
                    SCRF[:, 512:1024], E, TBL[:, tb_s:tb_s + 512], ALU.mult)),
                    inc=None)
                p.add("vector", (lambda e, E=E, O=O, tb_s=tb_s: e.tensor_tensor(
                    E, O, TBL[:, tb_s:tb_s + 512], ALU.mult)), inc=None)
                p.add("vector", (lambda e, E=E: e.tensor_tensor(
                    E, SCRF[:, 0:512], E, ALU.subtract)), inc=None)
                p.add("vector", (lambda e, O=O, tb_c=tb_c: e.tensor_tensor(
                    SCRF[:, 0:512], O, TBL[:, tb_c:tb_c + 512], ALU.mult)), inc=None)
                p.add("vector", (lambda e, O=O: e.tensor_tensor(
                    O, SCRF[:, 0:512], SCRF[:, 512:1024], ALU.add)), inc=(SV, 1))
                rope_done_v[mtile] = V()
                tbl_free[r] = rope_done_v[mtile]

        if n == 3:
            for mtile in range(NT):
                p.add("scalar", (lambda e, mtile=mtile: e.activation(
                    X32[:, 0:D], A2[:, mtile * D:(mtile + 1) * D], AF.Square,
                    accum_out=STAT[:, C_SS:C_SS + 1])),
                    waits=[rope_done_v[mtile], kevac3_s[mtile], x0_free[NT - 1],
                           rstdk_v.get(mtile - 1)],
                    inc=None)
                p.add("scalar", lambda e: e.activation(
                    STAT[:, C_SS:C_SS + 1], STAT[:, C_SS:C_SS + 1], AF.Ln,
                    bias=0.0, scale=1.0), inc=None)
                p.add("scalar", lambda e: e.activation(
                    STAT[:, C_SS:C_SS + 1], STAT[:, C_SS:C_SS + 1], AF.Exp,
                    bias=0.0, scale=-0.5), inc=(SS, 1))
                rk_s = S()
                p.add("vector", lambda e: e.tensor_scalar(
                    STAT[:, C_SS:C_SS + 1], STAT[:, C_SS:C_SS + 1], 1e12, None,
                    ALU.min), waits=[rk_s], inc=None)
                p.add("vector", (lambda e, mtile=mtile: e.tensor_copy(
                    STATB[:, CB_RSTDK + mtile:CB_RSTDK + mtile + 1],
                    STAT[:, C_SS:C_SS + 1])), inc=(SV, 1))
                rstdk_v[mtile] = V()
            ptmp_free[0] = tbl_free[0]
            ptmp_free[1] = tbl_free[0]

    if limit <= 2:
        return _finish()
    pa_done = P()
    tbl_free[0] = pa_done
    tbl_free[1] = pa_done
    p.add("vector", lambda e: e.tensor_copy(ASB[:], PA[:, 0:16]),
          waits=[pa_done], inc=(SV, 1))
    apart_v = V()
    p.add("sync", lambda e: e.dma_start(out=arin, in_=ASB[:]),
          waits=[apart_v], inc=(sar, 16))
    arin_d = T(sar)
    p.add("gpsimd", lambda e: e.collective_compute(
        "AllReduce", ALU.add, replica_groups=[[0, 1, 2, 3], [4, 5, 6, 7]],
        ins=[arin], outs=[arout]), waits=[arin_d], inc=(CC, 1))
    p.add("sync", lambda e: e.dma_start(out=ASB[:], in_=arout),
          waits=[(CC, 1), apart_v], inc=(sar, 16))
    asb_d = T(sar)
    if dbg:
        dbgA = nc.dram_tensor("dbgA", [128, 16], F32, kind="ExternalOutput").ap()
        p.add("sync", lambda e: e.dma_start(out=dbgA, in_=ASB[:]),
              waits=[asb_d], inc=(sar, 16))
    if limit <= 3:
        return _finish()

    a2_last_read = V()
    p.add("gpsimd", lambda e: e.dma_start(out=A2[:, :], in_=woutd),
        waits=[a2_last_read, wkv_buf_free[0], wkv_buf_free[1]], inc=(sgwo, 16))
    wout_g = T(sgwo)

    QWIN = 4096
    mq_order = [0, 4, 1, 5, 2, 6, 3, 7] + list(range(8, 16))
    wq_buf_free = {0: None, 1: None}
    qwin_free = {0: None, 1: None, 2: None}
    qev_s = {}
    spill_d = {}
    last_spill = None

    def slot_of(mq):
        return 0 if mq < 4 else (1 if mq < 8 else 2)

    for idx, mq in enumerate(mq_order):
        b = idx % 2
        wq_load_h = []
        for hh in range(2):
            p.add("gpsimd", (lambda e, mq=mq, b=b, hh=hh: e.dma_start(
                out=WKV[:, b * 2048 + hh * 1024:b * 2048 + (hh + 1) * 1024],
                in_=wqd[mq][:, hh * 1024:(hh + 1) * 1024])),
                waits=([allffn1_p, wkv_buf_free[0], wkv_buf_free[1],
                        wq_buf_free[b]] if hh == 0 else [T(sgwq[b])]),
                inc=(sgwq[b], 16))
            wq_load_h.append(T(sgwq[b]))
        wq_load = wq_load_h[1]
        for k in range(NK):
            for sl in range(4):
                p.add("tensor", (lambda e, k=k, sl=sl, b=b: e.matmul(
                    PF[sl][:, :], WKV[:, b * 2048 + k * 128:b * 2048 + (k + 1) * 128],
                    A1[:, k * D + sl * 512:k * D + (sl + 1) * 512],
                    start=(k == 0), stop=(k == NK - 1))),
                    waits=[wq_load_h[k // 8]] + ([pf_w(PF[sl])] if k == 0 else []),
                    inc=(SP, 1) if (k == NK - 1 and sl == 3) else None)
        wq_buf_free[b] = P()
        qmm_p = P()
        w = slot_of(mq)
        for sl in range(4):
            p.add("scalar", (lambda e, sl=sl, w=w, mq=mq: e.activation(
                WKV[:, QWIN + w * 2048 + sl * 512:QWIN + w * 2048 + (sl + 1) * 512],
                PF[sl][:, :], AF.Identity, bias=BQ[:, mq:mq + 1], scale=1.0)),
                waits=[qmm_p, qwin_free[w]] if sl == 0 else [], inc=(SS, 1))
            pf_free[id(PF[sl])] = S()
        qev_s[mq] = S()

        if 4 <= mq < 8:
            c = mq - 4
            last_v = None
            for sl4 in range(4):
                r = sl4 % 2
                tb_c = r * 1024
                tb_s = r * 1024 + 512
                p.add("sync", (lambda e, c=c, sl4=sl4, tb_c=tb_c: e.dma_start(
                    out=TBL[:, tb_c:tb_c + 1024].rearrange("p (h o) -> p h o", h=2),
                    in_=tfeatd[c * 128:(c + 1) * 128, :].rearrange(
                        "p (h t) -> p h t", h=2)[:, :, sl4 * 512:(sl4 + 1) * 512])),
                    waits=[tbl_free[r]], inc=(sgt[r], 16))
                tbl_d = T(sgt[r])
                E = WKV[:, QWIN + sl4 * 512:QWIN + (sl4 + 1) * 512]
                O = WKV[:, QWIN + 2048 + sl4 * 512:QWIN + 2048 + (sl4 + 1) * 512]
                p.add("vector", (lambda e, E=E, tb_c=tb_c: e.tensor_tensor(
                    SCRF[:, 0:512], E, TBL[:, tb_c:tb_c + 512], ALU.mult)),
                    waits=[tbl_d, qev_s[c], qev_s[c + 4]], inc=None)
                p.add("vector", (lambda e, E=E, tb_s=tb_s: e.tensor_tensor(
                    SCRF[:, 512:1024], E, TBL[:, tb_s:tb_s + 512], ALU.mult)), inc=None)
                p.add("vector", (lambda e, E=E, O=O, tb_s=tb_s: e.tensor_tensor(
                    E, O, TBL[:, tb_s:tb_s + 512], ALU.mult)), inc=None)
                p.add("vector", (lambda e, E=E: e.tensor_tensor(
                    E, SCRF[:, 0:512], E, ALU.subtract)), inc=None)
                p.add("vector", (lambda e, O=O, tb_c=tb_c: e.tensor_tensor(
                    SCRF[:, 0:512], O, TBL[:, tb_c:tb_c + 512], ALU.mult)), inc=None)
                p.add("vector", (lambda e, O=O: e.tensor_tensor(
                    O, SCRF[:, 0:512], SCRF[:, 512:1024], ALU.add)), inc=(SV, 1))
                tbl_free[r] = V()
                last_v = V()
            for (w2, ch) in ((0, c), (1, c + 4)):
                p.add("vector", (lambda e, w2=w2, ch=ch: e.tensor_scalar(
                    WKV[:, QWIN + w2 * 2048:QWIN + (w2 + 1) * 2048],
                    WKV[:, QWIN + w2 * 2048:QWIN + (w2 + 1) * 2048],
                    ASB[:, ch:ch + 1], None, ALU.mult)),
                    waits=[asb_d, last_v], inc=(SV, 1))
                yv = V()
                for mg4 in range(4):
                    p.add("sync", (lambda e, w2=w2, ch=ch, mg4=mg4: e.dma_start(
                        out=ytsp[mg4 * 4:(mg4 + 1) * 4, :,
                                 ch * 128:(ch + 1) * 128].rearrange(
                            "m p t -> p m t"),
                        in_=WKV[:, QWIN + w2 * 2048 + mg4 * 512:
                                QWIN + w2 * 2048 + (mg4 + 1) * 512].rearrange(
                            "p (m t) -> p m t", m=4),
                        allow_slow_non_contiguous=True)),
                        waits=[yv] if mg4 == 0 else [T(sspy[w2])],
                        inc=(sspy[w2], 16))
                spill_d[ch] = T(sspy[w2])
                qwin_free[w2] = spill_d[ch]
                last_spill = spill_d[ch]
        elif mq >= 8:
            p.add("vector", (lambda e, mq=mq: e.tensor_scalar(
                WKV[:, QWIN + 2 * 2048:QWIN + 3 * 2048],
                WKV[:, QWIN + 2 * 2048:QWIN + 3 * 2048],
                ASB[:, mq:mq + 1], None, ALU.mult)),
                waits=[asb_d, qev_s[mq]], inc=(SV, 1))
            yv = V()
            for mg4 in range(4):
                p.add("sync", (lambda e, mq=mq, mg4=mg4: e.dma_start(
                    out=ytsp[mg4 * 4:(mg4 + 1) * 4, :,
                             mq * 128:(mq + 1) * 128].rearrange("m p t -> p m t"),
                    in_=WKV[:, QWIN + 2 * 2048 + mg4 * 512:
                            QWIN + 2 * 2048 + (mg4 + 1) * 512].rearrange(
                        "p (m t) -> p m t", m=4),
                    allow_slow_non_contiguous=True)),
                    waits=[yv] if mg4 == 0 else [T(sspy[2])],
                    inc=(sspy[2], 16))
            spill_d[mq] = T(sspy[2])
            qwin_free[2] = spill_d[mq]
            last_spill = spill_d[mq]

    q_done_p = P()
    if limit <= 4:
        return _finish()
    p.add("sync", lambda e: e.dma_start(out=A1[:, :], in_=wfsp),
          waits=[q_done_p], inc=(swsp, 16))
    wft_d = T(swsp)

    YT0 = 0
    X1R = 8192
    Q2T = 12288
    yt_free = {0: None, 1: None}
    x1r_free = {0: None, 1: None}
    tmpb_p2_free = q_done_p
    x32_p2_free = {0: None, 1: None}
    out_d = None
    ff16_free = {0: None, 1: None}
    stage_b_state = {}
    stage_a_state = {}

    load_state = {}

    def load_a(m):
        r = m % 2
        p.add("sync", (lambda e, m=m, r=r: e.dma_start(
            out=WKV[:, YT0 + r * 2048:YT0 + (r + 1) * 2048],
            in_=ytsp[m])),
            waits=([T(sspy[0]), T(sspy[1]), T(sspy[2]), yt_free[r],
                    wq_buf_free[0], wq_buf_free[1]]
                   if m < 2 else [yt_free[r]]),
            inc=(syt[r], 16))
        yt_d = T(syt[r])
        p.add("sync", (lambda e, m=m, r=r: e.dma_start(
            out=WKV[:, X1R + r * 2048:X1R + (r + 1) * 2048],
            in_=x1sp[m * 128:(m + 1) * 128, :])),
            waits=[T(sspy[0]), T(sspy[1]), T(sspy[2]), x1r_free[r]]
            if m < 2 else [x1r_free[r]],
            inc=(sx1r[r], 16))
        x1_d = T(sx1r[r])
        load_state[m] = (yt_d, x1_d)

    def stage_a(m):
        nonlocal tmpb_p2_free, out_d
        r = m % 2
        yt_d, x1_d = load_state.pop(m)
        x2 = X32[:, r * D:(r + 1) * D]
        p.add("scalar", (lambda e, m=m, r=r, x2=x2: e.activation(
            x2, WKV[:, X1R + r * 2048:X1R + (r + 1) * 2048], AF.Identity,
            bias=STAT[:, C_NM2 + m:C_NM2 + m + 1],
            scale=STAT[:, C_R2P1 + m:C_R2P1 + m + 1])),
            waits=[x1_d, x32_p2_free[r]], inc=(SS, 1))
        x2s_s = S()
        defer = []
        for n in range(4):
            bank = PF[n % 2]
            for k in range(NK):
                p.add("tensor", (lambda e, bank=bank, k=k, n=n, r=r: e.matmul(
                    bank[:, :],
                    WKV[:, YT0 + r * 2048 + k * 128:YT0 + r * 2048 + (k + 1) * 128],
                    A2[:, k * D + n * 512:k * D + (n + 1) * 512],
                    start=(k == 0), stop=False)),
                    waits=[yt_d, wout_g, pf_w(bank)] if k == 0 else [], inc=None)
            p.add("tensor", (lambda e, bank=bank, n=n: e.matmul(
                bank[:, :], R_ONES128A, R_BO[0:1, n * 512:(n + 1) * 512],
                start=False, stop=True)), inc=(SP, 1))
            wmm_p = P()
            if n < 2:
                p.add("vector", (lambda e, bank=bank, n=n, x2=x2: e.tensor_tensor(
                    x2[:, n * 512:(n + 1) * 512], x2[:, n * 512:(n + 1) * 512],
                    bank[:, :], ALU.add)),
                    waits=[wmm_p, x2s_s], inc=(SV, 1))
                pf_free[id(bank)] = V()
            else:
                defer.append((bank, wmm_p, n))
        yt_free[r] = P()
        stage_a_state[m] = (r, x2, x2s_s, defer)

    def stage_a2(m):
        r, x2, x2s_s, defer = stage_a_state.pop(m)
        for bank, wmm_p, n in defer:
            p.add("vector", (lambda e, bank=bank, n=n, x2=x2: e.tensor_tensor(
                x2[:, n * 512:(n + 1) * 512], x2[:, n * 512:(n + 1) * 512],
                bank[:, :], ALU.add)),
                waits=[wmm_p, x2s_s], inc=(SV, 1))
            pf_free[id(bank)] = V()
        x2_v = V()
        x1r_free[r] = x2_v
        stage_b_state[m] = (r, x2, x2_v)

    def stage_b1(m):
        r, x2, x2_v = stage_b_state[m]
        for c in range(4):
            p.add("vector", (lambda e, c=c, x2=x2: e.bn_stats(
                BNS[:, c * 6:(c + 1) * 6], x2[:, c * 512:(c + 1) * 512])), inc=None)
        p.add("vector", lambda e: e.bn_aggr(BNS[:, 24:26], BNS[:, 0:24]), inc=None)
        p.add("vector", lambda e: e.tensor_scalar(
            STAT[:, C_RSTD:C_RSTD + 1], BNS[:, 25:26], LN_EPS, None, ALU.add),
            inc=None)
        p.add("vector", (lambda e, x2=x2: e.tensor_reduce(
            STAT[:, C_TA:C_TA + 1], x2, AX, ALU.max)), inc=None)
        p.add("vector", (lambda e, x2=x2: e.tensor_reduce(
            STAT[:, C_TB:C_TB + 1], x2, AX, ALU.min)), inc=(SV, 1))
        st3_v = V()
        p.add("scalar", lambda e: e.activation(
            STAT[:, C_RSTD:C_RSTD + 1], STAT[:, C_RSTD:C_RSTD + 1], AF.Ln,
            bias=0.0, scale=1.0), waits=[st3_v], inc=None)
        p.add("scalar", lambda e: e.activation(
            STAT[:, C_RSTD:C_RSTD + 1], STAT[:, C_RSTD:C_RSTD + 1], AF.Exp,
            bias=0.0, scale=-0.5), inc=(SS, 1))
        r3_s = S()
        p.add("vector", (lambda e, r=r: e.tensor_copy(
            STAT[:, C_M3 + r:C_M3 + r + 1], BNS[:, 24:25])), waits=[r3_s], inc=None)
        p.add("vector", lambda e: e.tensor_tensor(
            STAT[:, C_TA:C_TA + 1], STAT[:, C_TA:C_TA + 1], BNS[:, 24:25],
            ALU.subtract), inc=None)
        p.add("vector", lambda e: e.tensor_tensor(
            STAT[:, C_TB:C_TB + 1], BNS[:, 24:25], STAT[:, C_TB:C_TB + 1],
            ALU.subtract), inc=None)
        p.add("vector", lambda e: e.tensor_tensor(
            STAT[:, C_TA:C_TA + 1], STAT[:, C_TA:C_TA + 1], STAT[:, C_TB:C_TB + 1],
            ALU.max), inc=None)
        p.add("vector", lambda e: e.tensor_tensor(
            STAT[:, C_TA:C_TA + 1], STAT[:, C_TA:C_TA + 1], STAT[:, C_RSTD:C_RSTD + 1],
            ALU.mult), inc=None)
        p.add("vector", lambda e: e.tensor_scalar(
            STAT[:, C_TA:C_TA + 1], STAT[:, C_TA:C_TA + 1], Q_EPS, None, ALU.max),
            inc=None)
        p.add("vector", (lambda e, r=r: e.tensor_scalar(
            STAT[:, C_ISH3 + r:C_ISH3 + r + 1], STAT[:, C_TA:C_TA + 1],
            0.5 / 127.0, None, ALU.mult)), inc=None)
        p.add("vector", lambda e: e.reciprocal(
            STAT[:, C_TB:C_TB + 1], STAT[:, C_TA:C_TA + 1]), inc=None)
        p.add("vector", lambda e: e.tensor_scalar(
            STAT[:, C_TB:C_TB + 1], STAT[:, C_TB:C_TB + 1], 127.0, None, ALU.mult),
            inc=None)
        p.add("vector", (lambda e, r=r: e.tensor_tensor(
            STAT[:, C_G3 + r:C_G3 + r + 1], STAT[:, C_TB:C_TB + 1],
            STAT[:, C_RSTD:C_RSTD + 1], ALU.mult)), inc=None)
        p.add("vector", (lambda e, r=r: e.tensor_copy(
            STATB[:, CB_H2 + r:CB_H2 + r + 1], STAT[:, C_TB:C_TB + 1])), inc=(SV, 1))
        h2c_v = V()
        p.add("vector", (lambda e, x2=x2, r=r: e.tensor_scalar(
            QI8[:], x2, STAT[:, C_M3 + r:C_M3 + r + 1],
            STAT[:, C_G3 + r:C_G3 + r + 1], ALU.subtract, ALU.mult)), inc=None)
        p.add("vector", lambda e: e.tensor_copy(TBL[:, 0:D], QI8[:]),
              waits=[tmpb_p2_free], inc=(SV, 1))
        q3_v = V()
        stage_b_state[m] = (r, x2, h2c_v, q3_v)

    def stage_b2(m):
        nonlocal tmpb_p2_free, out_d
        r, x2, h2c_v, q3_v = stage_b_state.pop(m)
        ptb = PT[(m + 1) % 2]
        p.add("tensor", (lambda e, ptb=ptb, r=r: e.transpose(
            ptb[0:1, 0:128], STATB[:, CB_H2 + r:CB_H2 + r + 1], IDENT[:])),
            waits=[pt_w(ptb), h2c_v], inc=(SP, 1))
        p.add("scalar", (lambda e, ptb=ptb, m=m: e.copy(R_H2(m), ptb[0:1, 0:128])),
              waits=[P()], inc=(SS, 1))
        pt_free[id(ptb)] = S()
        h2r_s = S()

        evac_s = []
        for half in range(2):
            ptb = PT[half]
            for j in range(8):
                k = half * 8 + j
                p.add("tensor", (lambda e, ptb=ptb, j=j, k=k: e.transpose(
                    ptb[:, j * 128:(j + 1) * 128], TBL[:, k * 128:(k + 1) * 128],
                    IDENT[:])),
                    waits=[pt_w(ptb), q3_v] if j == 0 else [],
                    inc=(SP, 1) if j == 7 else None)
            tr_p = P()
            for j in range(8):
                k = half * 8 + j
                p.add("scalar", (lambda e, ptb=ptb, j=j, k=k: e.copy(
                    WKV[:, Q2T + k * 128:Q2T + (k + 1) * 128],
                    ptb[:, j * 128:(j + 1) * 128])),
                    waits=[tr_p] if j == 0 else [], inc=(SS, 1) if j == 7 else None)
                evac_s.append(S())
            pt_free[id(ptb)] = S()
        tmpb_p2_free = P()

        x3_vs = {}
        for n in range(4):
            bank = PF[2 + n % 2]
            for k in range(NK):
                p.add("tensor", (lambda e, bank=bank, k=k, n=n: e.matmul(
                    bank[:, :], WKV[:, Q2T + k * 128:Q2T + (k + 1) * 128],
                    A1[:, k * D + n * 512:k * D + (n + 1) * 512],
                    start=(k == 0), stop=False)),
                    waits=[evac_s[k], wft_d] + ([pf_w(bank)] if k == 0 else []),
                    inc=None)
            p.add("tensor", (lambda e, bank=bank, n=n, m=m: e.matmul(
                bank[:, :], R_H2(m), R_FB[0:1, n * 512:(n + 1) * 512],
                start=False, stop=True)),
                waits=[h2r_s], inc=(SP, 1))
            f2_p = P()
            sl = n % 2
            p.add("scalar", (lambda e, bank=bank, sl=sl, r=r: e.activation(
                FF16[:, sl * 512:(sl + 1) * 512], bank[:, :], AF.Identity,
                bias=0.0, scale=STAT[:, C_ISH3 + r:C_ISH3 + r + 1])),
                waits=[f2_p, ff16_free[sl]], inc=(SS, 1))
            pf_free[id(bank)] = S()
            ff2_s = S()
            p.add("vector", (lambda e, sl=sl, n=n, x2=x2: e.tensor_tensor(
                x2[:, n * 512:(n + 1) * 512], x2[:, n * 512:(n + 1) * 512],
                FF16[:, sl * 512:(sl + 1) * 512], ALU.add)),
                waits=[ff2_s], inc=(SV, 1))
            ff16_free[sl] = V()
            x3_vs[n] = V()
        p.add("sync", (lambda e, m=m, x2=x2: e.dma_start(
            out=outd[m * 128:(m + 1) * 128, :], in_=x2)),
            waits=[x3_vs[3]], inc=(sout[r], 16))
        out_d = T(sout[r])
        x32_p2_free[r] = out_d

    load_a(0)
    for m in range(p2_tiles):
        if m + 1 < p2_tiles:
            load_a(m + 1)
        stage_a(m)
        if m >= 1:
            stage_b1(m - 1)
        stage_a2(m)
        if m >= 1:
            stage_b2(m - 1)
    stage_b1(p2_tiles - 1)
    stage_b2(p2_tiles - 1)

    p.add("sync", lambda e: e.nop(), waits=[T(sout[0]), T(sout[1])])
    return _finish()


_theta = 10000.0


def _tables(core):
    pos = (core % 4) * TPC + np.arange(TPC, dtype=np.float64)
    j = np.arange(NPAIR, dtype=np.float64)
    inv = _theta ** (-2.0 * j / 1024.0)
    ang = pos[:, None] * inv[None, :]
    return np.cos(ang).astype(np.float32), np.sin(ang).astype(np.float32)


def kernel(L0_attn_X, L0_attn_W_in, L0_attn_b_in, L0_attn_W_out, L0_attn_b_out,
           L0_ffn_weight, L0_ffn_w_scale, L0_ffn_bias):
    X = np.asarray(L0_attn_X)
    W_in = np.asarray(L0_attn_W_in)
    b_in = np.asarray(L0_attn_b_in)
    W_out = np.asarray(L0_attn_W_out)
    b_out = np.asarray(L0_attn_b_out)
    Wf = np.asarray(L0_ffn_weight)
    ws = np.asarray(L0_ffn_w_scale)
    fb = np.asarray(L0_ffn_bias)

    wq0 = W_in[0:D][PERM, :].T
    wq = np.ascontiguousarray(
        wq0.reshape(16, 128, 16, 128).transpose(2, 1, 0, 3).reshape(16, 128, D))
    wkv0 = np.concatenate([W_in[D:2 * D][PERM, :], W_in[2 * D:3 * D][PERM, :]],
                          axis=0).T
    wkv = np.ascontiguousarray(
        wkv0.reshape(16, 128, 8, 512).transpose(2, 1, 0, 3).reshape(8, 128, 8192))
    wout0 = W_out[:, PERM].T
    wout = np.ascontiguousarray(
        wout0.reshape(16, 128, D).transpose(1, 0, 2).reshape(128, NK * D))
    wf = np.ascontiguousarray(
        Wf.T.reshape(16, 128, D).transpose(1, 0, 2).reshape(128, NK * D))
    wsr = np.ascontiguousarray(ws.reshape(1, D))
    bq = np.ascontiguousarray(b_in[0:D][PERM])
    bk = np.ascontiguousarray(b_in[D:2 * D][PERM].reshape(1, D))
    bv = np.ascontiguousarray(b_in[2 * D:3 * D][PERM].reshape(1, D))
    bo = np.ascontiguousarray(b_out.reshape(1, D))
    fbr = np.ascontiguousarray(fb.reshape(1, D))
    ident = np.eye(128, dtype=BF)
    ones32 = np.ones((1, 128), dtype=np.float32)

    nc = build_program()
    in_maps = []
    Xf = np.ascontiguousarray(X.reshape(8, TPC, D))
    for c in range(8):
        ct, st = _tables(c)
        in_maps.append({
            "x0": Xf[c], "wq": wq, "wkv": wkv, "wout": wout, "wf": wf,
            "ws": wsr, "bq": bq, "bk": bk, "bv": bv, "bo": bo, "fb": fbr,
            "ttok": np.ascontiguousarray(
                np.concatenate([ct, st], axis=1).astype(BF)),
            "tfeat": np.ascontiguousarray(
                np.concatenate([ct.T, st.T], axis=1).astype(BF)),
            "ident": ident, "ones32": ones32,
        })
    res = run_bass_kernel_spmd(nc, in_maps, core_ids=list(range(8)))
    out = np.stack([res.results[c]["out"] for c in range(8)], axis=0)
    return np.ascontiguousarray(out.reshape(2, 8192, D))


# version 18 (final)
# speedup vs baseline: 1.2129x; 1.2129x over previous
import numpy as np
import ml_dtypes
import concourse.bass as bass
import concourse.mybir as mybir
from concourse.bass_utils import run_bass_kernel_spmd

F32 = mybir.dt.float32
BF16 = mybir.dt.bfloat16
F16 = mybir.dt.float16
I8 = mybir.dt.int8
AF = mybir.ActivationFunctionType
ALU = mybir.AluOpType
AX = mybir.AxisListType.X
BF = ml_dtypes.bfloat16

D = 2048
NT = 16
TPC = NT * 128
NK = 16
ST = 2
NPAIR = 512
LN_EPS = 1e-5
Q_EPS = 1e-7

PERM = np.concatenate([np.arange(0, 1024, 2), np.arange(1, 1024, 2),
                       np.arange(1024, 2048)]).astype(np.int64)

ENGINES = ("sync", "scalar", "vector", "tensor", "gpsimd")


class _Sem:
    __slots__ = ("name", "n", "h")

    def __init__(self, name):
        self.name = name
        self.n = 0
        self.h = None


class _Op:
    __slots__ = ("fn", "waits", "inc", "self_wait")

    def __init__(self, fn, waits, inc, self_wait):
        self.fn = fn
        self.waits = waits
        self.inc = inc
        self.self_wait = self_wait


class _Plan:
    def __init__(self):
        self.ops = {e: [] for e in ENGINES}
        self.sems = []
        self.auto = {}

    def sem(self, name):
        s = _Sem(name)
        self.sems.append(s)
        return s

    def add(self, eng, fn, waits=(), inc=None):
        ws = [w for w in waits if w is not None]
        self_wait = None
        if eng in self.auto:
            a = self.auto[eng]
            if inc is None:
                inc = (a, 1)
            if a.n > 0:
                self_wait = (a, a.n)
        self.ops[eng].append(_Op(fn, ws, inc, self_wait))
        if inc is not None:
            inc[0].n += inc[1]

    def emit(self, block):
        last = {}

        def mk(ename):
            def run(eng):
                for op in self.ops[ename]:
                    sw = [op.self_wait] if op.self_wait is not None else []
                    for (s, v) in list(op.waits) + sw:
                        if v <= 0:
                            continue
                        key = (ename, id(s))
                        if last.get(key, 0) < v:
                            eng.wait_ge(s.h, v)
                            last[key] = v
                    inst = op.fn(eng)
                    if op.inc is not None:
                        inst.then_inc(op.inc[0].h, op.inc[1])
            return run

        block.sync(mk("sync"))
        block.scalar(mk("scalar"))
        block.vector(mk("vector"))
        block.tensor(mk("tensor"))
        block.gpsimd(mk("gpsimd"))


def build_program(limit=99, limit2=99, p2_tiles=NT, dbg=False):
    nc = bass.Bass("TRN2", target_bir_lowering=False, debug=False)

    x0d = nc.dram_tensor("x0", [TPC, D], F32, kind="ExternalInput").ap()
    wqd = nc.dram_tensor("wq", [16, 128, D], F32, kind="ExternalInput").ap()
    wkvd = nc.dram_tensor("wkv", [8, 128, 8192], F32, kind="ExternalInput").ap()
    woutd = nc.dram_tensor("wout", [128, NK * D], F32, kind="ExternalInput").ap()
    wfd = nc.dram_tensor("wf", [128, NK * D], I8, kind="ExternalInput").ap()
    wsd = nc.dram_tensor("ws", [1, D], F32, kind="ExternalInput").ap()
    bqd = nc.dram_tensor("bq", [D], F32, kind="ExternalInput").ap()
    bkd = nc.dram_tensor("bk", [1, D], F32, kind="ExternalInput").ap()
    bvd = nc.dram_tensor("bv", [1, D], F32, kind="ExternalInput").ap()
    bod = nc.dram_tensor("bo", [1, D], F32, kind="ExternalInput").ap()
    fbd = nc.dram_tensor("fb", [1, D], F32, kind="ExternalInput").ap()
    ttokd = nc.dram_tensor("ttok", [TPC, 2 * NPAIR], BF16, kind="ExternalInput").ap()
    tfeatd = nc.dram_tensor("tfeat", [NPAIR, 2 * TPC], BF16, kind="ExternalInput").ap()
    identd = nc.dram_tensor("ident", [128, 128], BF16, kind="ExternalInput").ap()
    ones32d = nc.dram_tensor("ones32", [1, 128], F32, kind="ExternalInput").ap()
    outd = nc.dram_tensor("out", [TPC, D], F32, kind="ExternalOutput").ap()

    ik = "ExternalOutput" if dbg else "Internal"
    x1sp = nc.dram_tensor("x1sp", [TPC, D], BF16, kind=ik).ap()
    wfsp = nc.dram_tensor("wfsp", [128, NK * D], BF16).ap()
    ytsp = nc.dram_tensor("ytsp", [NT, 128, NK * 128], BF16).ap()
    arin = nc.dram_tensor("arin", [128, 16], F32).ap()
    arout = nc.dram_tensor("arout", [128, 16], F32).ap()

    p = _Plan()
    SV = p.sem("sv")
    SS = p.sem("ss")
    SP = p.sem("sp")
    CC = p.sem("cc")
    p.auto = {"vector": SV, "scalar": SS}
    sdc = p.sem("sdc")
    swsp = p.sem("swsp")
    sar = p.sem("sar")
    sspy = [p.sem("sspy0"), p.sem("sspy1"), p.sem("sspy2")]
    sgb = p.sem("sgb")
    sgwf = p.sem("sgwf")
    sgwo = p.sem("sgwo")
    sx0 = [p.sem("sx0a"), p.sem("sx0b"), p.sem("sx0c"),
           p.sem("sx0d"), p.sem("sx0e")]
    sx1f = p.sem("sx1f")
    sx1s = [p.sem("sx1sa"), p.sem("sx1sb")]
    syt = [p.sem("syta"), p.sem("sytb")]
    sx1r = [p.sem("sx1ra"), p.sem("sx1rb")]
    sout = [p.sem("souta"), p.sem("soutb")]
    sgwkv = [p.sem("sgwkva"), p.sem("sgwkvb")]
    sgwq = [p.sem("sgwqa"), p.sem("sgwqb"), p.sem("sgwqc")]
    sgt = [p.sem("sgta"), p.sem("sgtb")]

    def V(): return (SV, SV.n)
    def S(): return (SS, SS.n)
    def P(): return (SP, SP.n)
    def T(s): return (s, s.n)

    guards = []

    def sbuf(name, shape, dt):
        g = nc.sbuf_tensor(name, shape, dt)
        guards.append(g)
        return g.__enter__()

    def psum(name, shape, dt):
        g = nc.psum_tensor(name, shape, dt)
        guards.append(g)
        return g.__enter__()

    A1 = sbuf("A1", [128, NK * D], BF16)
    A2 = sbuf("A2", [128, NK * D], BF16)
    WKV = sbuf("WKV", [128, 8 * D], BF16)
    X32 = sbuf("X32", [128, 2 * D], F32)
    TBL = sbuf("TBL", [128, 2048], BF16)
    FF16 = sbuf("FF16", [128, 1024], F16)
    QI8 = sbuf("QI8", [128, D], I8)
    BNS = sbuf("BNS", [128, 32], F32)
    STAT = sbuf("STAT", [128, 80], F32)
    STATB = sbuf("STATB", [128, 48], BF16)
    ROWS = sbuf("ROWS", [128, 4352], BF16)
    BQ = sbuf("BQ", [128, 16], F32)
    ASB = sbuf("ASB", [128, 16], F32)
    IDENT = sbuf("IDENT", [128, 128], BF16)
    IDENTF = sbuf("IDENTF", [128, 128], F32)
    SCRF = X32[:, 3072:4096]
    X0B = sbuf("X0B", [128, 3 * D], BF16)

    PT = [psum("PT0", [128, 1024], BF16), psum("PT1", [128, 1024], BF16)]
    PF = [psum("PF%d" % i, [128, 512], F32) for i in range(4)]
    PA = psum("PA", [128, 512], F32)
    PTF = psum("PTF", [128, 512], F32)

    def R_H(m):     return ROWS[0:1, (m % 4) * 128:(m % 4) * 128 + 128]
    def R_H2(m):    return ROWS[0:1, 512 + (m % 2) * 128:512 + (m % 2) * 128 + 128]
    def R_NM2(m):   return ROWS[0:1, 768 + (m % 2) * 128:768 + (m % 2) * 128 + 128]
    R_ONES512 = ROWS[0:1, 1024:1536]
    R_FB = ROWS[0:1, 1536:3584]
    R_ONES128A = ROWS[32:33, 0:128]
    R_BK = ROWS[32:33, 128:2176]
    R_BO = ROWS[32:33, 2176:4224]
    R_ONES128B = ROWS[64:65, 0:128]
    R_ZERO16 = ROWS[32:33, 4224:4240]
    R_BV = ROWS[64:65, 128:2176]
    X_WS = X32[0:1, 0:D]
    X_ONES32 = X32[0:1, D:D + 128]

    C_M1 = 0
    C_G = 2
    C_ISH = 4
    C_R2P1 = 8
    C_TA, C_TB, C_RSTD = 24, 25, 26
    C_M3 = 28
    C_G3 = 30
    C_ISH3 = 32
    C_SS = 34
    C_NM2 = 40
    C_R2 = 56
    CB_NM2R2 = 0
    CB_RSTDK = 16
    CB_H = 32
    CB_H2 = 36

    p.add("sync", lambda e: e.dma_start(out=IDENT[:], in_=identd), inc=(sdc, 16))
    p.add("gpsimd", lambda e: e.dma_start(out=IDENTF[:], in_=identd),
          inc=(sgb, 16))
    p.add("sync", lambda e: e.dma_start(out=X_WS, in_=wsd), inc=(sdc, 16))
    p.add("sync", lambda e: e.dma_start(out=X_ONES32, in_=ones32d), inc=(sdc, 16))
    p.add("sync", lambda e: e.dma_start(out=BQ[:], in_=bqd.rearrange("(c p) -> p c", p=128),
                                        allow_slow_non_contiguous=True),
          inc=(sdc, 16))
    consts_d = T(sdc)
    p.add("vector", lambda e: e.memset(R_ZERO16, 0.0), inc=(SV, 1))
    zero16_v = V()

    p.add("gpsimd", lambda e: e.dma_start(out=R_BK, in_=bkd), inc=(sgb, 16))
    p.add("gpsimd", lambda e: e.dma_start(out=R_BV, in_=bvd), inc=(sgb, 16))
    p.add("gpsimd", lambda e: e.dma_start(out=R_BO, in_=bod), inc=(sgb, 16))
    p.add("gpsimd", lambda e: e.dma_start(out=R_FB, in_=fbd), inc=(sgb, 16))
    p.add("gpsimd", lambda e: e.dma_start(out=R_ONES128A, in_=ones32d), inc=(sgb, 16))
    p.add("gpsimd", lambda e: e.dma_start(out=R_ONES128B, in_=ones32d), inc=(sgb, 16))
    for q4 in range(4):
        p.add("gpsimd", (lambda e, q4=q4: e.dma_start(
            out=ROWS[0:1, 1024 + q4 * 128:1024 + (q4 + 1) * 128], in_=ones32d)),
            inc=(sgb, 16))
    bias_g = T(sgb)
    p.add("gpsimd", lambda e: e.dma_start(out=A2[:, :], in_=wfd), inc=(sgwf, 16))
    wf_g = T(sgwf)

    ws_copy_v = {0: None, 1: None}
    for n in range(4):
        bank = PF[n % 2]
        p.add("tensor", (lambda e, b=bank, n=n: e.matmul(
            b[:, :], X_ONES32, X32[0:1, n * 512:(n + 1) * 512],
            start=True, stop=True)), waits=[consts_d, ws_copy_v[n % 2]],
            inc=(SP, 1))
        p.add("vector", (lambda e, b=bank, n=n: e.tensor_copy(
            X32[:, n * 512:(n + 1) * 512], b[:, :])), waits=[P()], inc=(SV, 1))
        ws_copy_v[n % 2] = V()
    for k in range(NK):
        p.add("vector", (lambda e, k=k: e.tensor_tensor(
            A2[:, k * D:(k + 1) * D], A2[:, k * D:(k + 1) * D], X32[:, 0:D], ALU.mult)),
            waits=[wf_g], inc=(SV, 1))
    fold_v = V()
    p.add("sync", lambda e: e.dma_start(out=wfsp, in_=A2[:, :]),
          waits=[fold_v], inc=(swsp, 16))
    wfsp_d = T(swsp)

    def _finish():
        p.add("sync", lambda e: e.nop(), waits=[V(), S(), P()])
        sem_guards = []
        for s in p.sems:
            g = nc.semaphore(s.name)
            sem_guards.append(g)
            s.h = g.__enter__()
        blk = nc.Block()
        block = blk.__enter__()
        p.emit(block)
        blk.__exit__(None, None, None)
        for g in reversed(sem_guards + guards):
            g.__exit__(None, None, None)
        return nc

    if limit <= 0:
        return _finish()

    x0_free = [None] * (NT + 4)
    x1_spill_d = [None] * NT
    nm2_v = [None] * NT
    q1t_read_p = [None] * (NT // ST + 4)
    tmpb_free = [None] * (NT + 4)
    ln2t_read_p = [None] * (NT + 4)
    pt_free = {}
    pf_free = {}
    ff16_free = {0: None, 1: None}
    qtbl_free = {0: None, 1: None}
    qbf_done_s = {0: None}
    q1t_evac = {}
    hrow_done = [None] * NT

    def pt_w(bank):
        return pt_free.get(id(bank))

    def pf_w(bank):
        return pf_free.get(id(bank))

    QB0 = 8192

    def x0ap(m, c0=0, c1=D):
        s = m % 5
        if s < 3:
            return X0B[:, s * D + c0:s * D + c1]
        if s == 3:
            return TBL[:, c0:c1]
        return X32[:, c0:c1]

    def ffn1_partA(st):
        stb = st % 2
        for m2 in range(ST):
            mt = st * ST + m2
            mpar = mt % 2
            for n in range(4):
                bank = PF[2 * (n % 2) + m2]
                for k in range(NK):
                    p.add("tensor", (lambda e, bank=bank, k=k, n=n, stb=stb, m2=m2:
                          e.matmul(bank[:, :],
                                   WKV[:, stb * 4096 + m2 * 2048 + k * 128:
                                       stb * 4096 + m2 * 2048 + (k + 1) * 128],
                                   A2[:, k * D + n * 512:k * D + (n + 1) * 512],
                                   start=(k == 0), stop=False)),
                        waits=[q1t_evac[mt][k]] +
                        ([fold_v, pf_w(bank), hrow_done[st * ST + ST - 1],
                          bias_g] if k == 0 else []),
                        inc=None)
                p.add("tensor", (lambda e, bank=bank, n=n, mt=mt: e.matmul(
                    bank[:, :], R_H(mt), R_FB[0:1, n * 512:(n + 1) * 512],
                    start=False, stop=True)),
                    inc=(SP, 1))
                mm_p = P()
                sl = n % 2
                mi4 = mt % 4
                p.add("scalar", (lambda e, bank=bank, sl=sl, mi4=mi4: e.activation(
                    FF16[:, sl * 512:(sl + 1) * 512], bank[:, :], AF.Identity,
                    bias=0.0, scale=STAT[:, C_ISH + mi4:C_ISH + mi4 + 1])),
                    waits=[mm_p, ff16_free[sl]], inc=(SS, 1))
                pf_free[id(bank)] = S()
                ffs = S()
                p.add("vector", (lambda e, sl=sl, mt=mt, n=n:
                      e.tensor_tensor(
                          x0ap(mt, n * 512, (n + 1) * 512),
                          x0ap(mt, n * 512, (n + 1) * 512),
                          FF16[:, sl * 512:(sl + 1) * 512], ALU.add)),
                    waits=[ffs], inc=(SV, 1))
                ff16_free[sl] = V()
            x1_v = V()
            sp_sem = sx1f if mt % 5 == 4 else sx1s[mpar]
            p.add("gpsimd" if mt % 5 == 4 else "sync",
                  (lambda e, mt=mt: e.dma_start(
                      out=x1sp[mt * 128:(mt + 1) * 128, :],
                      in_=x0ap(mt))),
                  waits=[x1_v], inc=(sp_sem, 16))
            x1_spill_d[mt] = T(sp_sem)
            for c in range(4):
                p.add("vector", (lambda e, c=c, mt=mt: e.bn_stats(
                    BNS[:, c * 6:(c + 1) * 6],
                    x0ap(mt, c * 512, (c + 1) * 512))), inc=None)
            p.add("vector", lambda e: e.bn_aggr(BNS[:, 24:26], BNS[:, 0:24]),
                  inc=None)
            p.add("vector", lambda e: e.tensor_scalar(
                STAT[:, C_RSTD:C_RSTD + 1], BNS[:, 25:26], LN_EPS, None, ALU.add),
                inc=(SV, 1))
            ln2st_v = V()
            p.add("scalar", lambda e: e.activation(
                STAT[:, C_RSTD:C_RSTD + 1], STAT[:, C_RSTD:C_RSTD + 1], AF.Ln,
                bias=0.0, scale=1.0), waits=[ln2st_v], inc=None)
            p.add("scalar", lambda e: e.activation(
                STAT[:, C_RSTD:C_RSTD + 1], STAT[:, C_RSTD:C_RSTD + 1], AF.Exp,
                bias=0.0, scale=-0.5), inc=(SS, 1))
            r2_s = S()
            p.add("vector", (lambda e, mt=mt: e.tensor_scalar(
                STAT[:, C_R2P1 + mt:C_R2P1 + mt + 1], STAT[:, C_RSTD:C_RSTD + 1],
                1.0, None, ALU.add)), waits=[r2_s], inc=None)
            p.add("vector", (lambda e, mt=mt: e.tensor_copy(
                STAT[:, C_R2 + mt:C_R2 + mt + 1], STAT[:, C_RSTD:C_RSTD + 1])),
                inc=None)
            p.add("vector", (lambda e, mt=mt: e.scalar_tensor_tensor(
                STAT[:, C_NM2 + mt:C_NM2 + mt + 1], BNS[:, 24:25], -1.0,
                STAT[:, C_RSTD:C_RSTD + 1], ALU.mult, ALU.mult)), inc=(SV, 1))
            nm2_v[mt] = V()
        q1t_read_p[st] = P()

    def ln2_partB(mt):
        p.add("scalar", (lambda e, mt=mt: e.activation(
            x0ap(mt), x0ap(mt),
            AF.Identity, bias=STAT[:, C_NM2 + mt:C_NM2 + mt + 1],
            scale=STAT[:, C_R2 + mt:C_R2 + mt + 1])),
            waits=[x1_spill_d[mt], nm2_v[mt]], inc=(SS, 1))
        ln2_s = S()
        if mt % 5 == 4:
            for b in range(4):
                for j in range(4):
                    k = b * 4 + j
                    p.add("tensor", (lambda e, j=j, k=k, mt=mt:
                          e.transpose(
                              PTF[:, j * 128:(j + 1) * 128],
                              x0ap(mt, k * 128, (k + 1) * 128),
                              IDENTF[:])),
                        waits=[pt_w(PTF), ln2_s, bias_g] if j == 0 else [],
                        inc=(SP, 1) if j == 3 else None)
                tr_p = P()
                p.add("scalar", (lambda e, b=b, mt=mt: e.copy(
                    A1[:, b * 4 * D:(b * 4 + 4) * D].rearrange(
                        "p (k c) -> p k c", c=D)[:, :, mt * 128:(mt + 1) * 128],
                    PTF[:, 0:512].rearrange("p (k c) -> p k c", c=128))),
                    waits=[tr_p], inc=(SS, 1))
                pt_free[id(PTF)] = S()
        else:
            for half in range(2):
                ptb = PT[half]
                for j in range(8):
                    k = half * 8 + j
                    p.add("tensor", (lambda e, ptb=ptb, j=j, k=k, mt=mt:
                          e.transpose(
                              ptb[:, j * 128:(j + 1) * 128],
                              x0ap(mt, k * 128, (k + 1) * 128),
                              IDENT[:])),
                        waits=[pt_w(ptb), ln2_s] if j == 0 else [],
                        inc=(SP, 1) if j == 7 else None)
                tr_p = P()
                p.add("scalar", (lambda e, ptb=ptb, half=half, mt=mt: e.copy(
                    A1[:, half * 8 * D:(half * 8 + 8) * D].rearrange(
                        "p (k c) -> p k c", c=D)[:, :, mt * 128:(mt + 1) * 128],
                    ptb[:, 0:1024].rearrange("p (k c) -> p k c", c=128))),
                    waits=[tr_p], inc=(SS, 1))
                pt_free[id(ptb)] = S()
        ln2t_read_p[mt] = P()
        x0_free[mt] = ln2t_read_p[mt]

    for m in range(NT):
        par = m % 2
        st = m // ST
        stb = st % 2
        mm = m % ST
        i4 = m % 4
        if m >= 4:
            ln2_partB(m - 4)

        if m % 5 == 4:
            p.add("sync", (lambda e, m=m: e.dma_start(
                out=x0ap(m), in_=x0d[m * 128:(m + 1) * 128, :])),
                waits=[x0_free[m - 5] if m >= 5 else fold_v],
                inc=(sx0[m % 5], 16))
        else:
            p.add("gpsimd", (lambda e, m=m: e.dma_start(
                out=x0ap(m), in_=x0d[m * 128:(m + 1) * 128, :])),
                waits=[x0_free[m - 5] if m >= 5 else None],
                inc=(sx0[m % 5], 16))
        x0_d = T(sx0[m % 5])

        for c in range(4):
            p.add("vector", (lambda e, c=c, m=m: e.bn_stats(
                BNS[:, c * 6:(c + 1) * 6], x0ap(m, c * 512, (c + 1) * 512))),
                waits=[x0_d] if c == 0 else [], inc=None)
        p.add("vector", lambda e: e.bn_aggr(BNS[:, 24:26], BNS[:, 0:24]), inc=None)
        p.add("vector", lambda e: e.tensor_scalar(
            STAT[:, C_RSTD:C_RSTD + 1], BNS[:, 25:26], LN_EPS, None, ALU.add),
            inc=None)
        p.add("vector", (lambda e, m=m: e.tensor_reduce(
            STAT[:, C_TA:C_TA + 1], x0ap(m), AX, ALU.max)), inc=None)
        p.add("vector", (lambda e, m=m: e.tensor_reduce(
            STAT[:, C_TB:C_TB + 1], x0ap(m), AX, ALU.min)), inc=(SV, 1))
        stats_v = V()
        p.add("scalar", lambda e: e.activation(
            STAT[:, C_RSTD:C_RSTD + 1], STAT[:, C_RSTD:C_RSTD + 1], AF.Ln,
            bias=0.0, scale=1.0), waits=[stats_v], inc=None)
        p.add("scalar", lambda e: e.activation(
            STAT[:, C_RSTD:C_RSTD + 1], STAT[:, C_RSTD:C_RSTD + 1], AF.Exp,
            bias=0.0, scale=-0.5), inc=(SS, 1))
        rstd1_s = S()
        p.add("vector", (lambda e, par=par: e.tensor_copy(
            STAT[:, C_M1 + par:C_M1 + par + 1], BNS[:, 24:25])),
            waits=[rstd1_s], inc=None)
        p.add("vector", lambda e: e.tensor_tensor(
            STAT[:, C_TA:C_TA + 1], STAT[:, C_TA:C_TA + 1], BNS[:, 24:25], ALU.subtract),
            inc=None)
        p.add("vector", lambda e: e.tensor_tensor(
            STAT[:, C_TB:C_TB + 1], BNS[:, 24:25], STAT[:, C_TB:C_TB + 1],
            ALU.subtract), inc=None)
        p.add("vector", lambda e: e.tensor_tensor(
            STAT[:, C_TA:C_TA + 1], STAT[:, C_TA:C_TA + 1], STAT[:, C_TB:C_TB + 1],
            ALU.max), inc=None)
        p.add("vector", lambda e: e.tensor_tensor(
            STAT[:, C_TA:C_TA + 1], STAT[:, C_TA:C_TA + 1], STAT[:, C_RSTD:C_RSTD + 1],
            ALU.mult), inc=None)
        p.add("vector", lambda e: e.tensor_scalar(
            STAT[:, C_TA:C_TA + 1], STAT[:, C_TA:C_TA + 1], Q_EPS, None, ALU.max),
            inc=None)
        p.add("vector", (lambda e, i4=i4: e.tensor_scalar(
            STAT[:, C_ISH + i4:C_ISH + i4 + 1], STAT[:, C_TA:C_TA + 1],
            0.5 / 127.0, None, ALU.mult)), inc=None)
        p.add("vector", lambda e: e.reciprocal(
            STAT[:, C_TB:C_TB + 1], STAT[:, C_TA:C_TA + 1]), inc=None)
        p.add("vector", lambda e: e.tensor_scalar(
            STAT[:, C_TB:C_TB + 1], STAT[:, C_TB:C_TB + 1], 127.0, None, ALU.mult),
            inc=None)
        p.add("vector", (lambda e, par=par: e.tensor_tensor(
            STAT[:, C_G + par:C_G + par + 1], STAT[:, C_TB:C_TB + 1],
            STAT[:, C_RSTD:C_RSTD + 1], ALU.mult)), inc=None)
        p.add("vector", (lambda e, i4=i4: e.tensor_copy(
            STATB[:, CB_H + i4:CB_H + i4 + 1], STAT[:, C_TB:C_TB + 1])), inc=(SV, 1))
        hcol_v = V()

        p.add("vector", (lambda e, m=m, par=par: e.tensor_scalar(
            QI8[:], x0ap(m), STAT[:, C_M1 + par:C_M1 + par + 1],
            STAT[:, C_G + par:C_G + par + 1], ALU.subtract, ALU.mult)),
            waits=[qbf_done_s[0]], inc=(SV, 1))
        quant_v = V()
        p.add("scalar", (lambda e, par=par: e.copy(
            WKV[:, QB0 + par * 2048:QB0 + (par + 1) * 2048], QI8[:])),
            waits=[quant_v, qtbl_free[par]], inc=(SS, 1))
        qbf_done_s[0] = S()
        qbf_v = S()

        ptb = PT[m % 2]
        p.add("tensor", (lambda e, ptb=ptb, i4=i4: e.transpose(
            ptb[0:1, 0:128], STATB[:, CB_H + i4:CB_H + i4 + 1], IDENT[:])),
            waits=[pt_w(ptb), hcol_v, consts_d], inc=(SP, 1))
        p.add("scalar", (lambda e, ptb=ptb, m=m: e.copy(R_H(m), ptb[0:1, 0:128])),
              waits=[P()], inc=(SS, 1))
        pt_free[id(ptb)] = S()
        hrow_done[m] = S()

        for half in range(2):
            ptb = PT[half]
            for j in range(8):
                k = half * 8 + j
                p.add("tensor", (lambda e, ptb=ptb, j=j, k=k, par=par: e.transpose(
                    ptb[:, j * 128:(j + 1) * 128],
                    WKV[:, QB0 + par * 2048 + k * 128:QB0 + par * 2048 + (k + 1) * 128],
                    IDENT[:])),
                    waits=[pt_w(ptb), qbf_v] if j == 0 else [],
                    inc=(SP, 1) if j == 7 else None)
            tr_p = P()
            p.add("scalar", (lambda e, ptb=ptb, half=half, stb=stb, mm=mm: e.copy(
                WKV[:, stb * 4096 + mm * 2048 + half * 1024:
                    stb * 4096 + mm * 2048 + (half + 1) * 1024],
                ptb[:, 0:1024])),
                waits=[tr_p, q1t_read_p[st - 2] if st >= 2 else None],
                inc=(SS, 1))
            ev_s = S()
            for j in range(8):
                q1t_evac.setdefault(m, [None] * NK)[half * 8 + j] = ev_s
            pt_free[id(ptb)] = ev_s
        qtbl_free[par] = P()
        q1t_s = S()

        if mm == 1 and st >= 1:
            ffn1_partA(st - 1)

    ln2_partB(NT - 4)
    ln2_partB(NT - 3)
    ffn1_partA(NT // ST - 1)
    ln2_partB(NT - 2)
    ln2_partB(NT - 1)

    lnt_done_s = S()
    allffn1_p = q1t_read_p[NT // ST - 1]
    if limit <= 1:
        return _finish()

    wkv_buf_free = {0: None, 1: None}
    ptmp_free = {0: None, 1: None}
    rstdk_v = {}
    tbl_free = {0: ln2t_read_p[NT - 1], 1: ln2t_read_p[NT - 1]}
    kevac1_s = {}
    kevac3_s = {}
    rope_done_v = {}

    for n in range(8):
        b = (n + 1) % 2
        kv_load_q = []
        for qq in range(4):
            p.add("gpsimd", (lambda e, n=n, b=b, qq=qq: e.dma_start(
                out=WKV[:, b * 8192 + qq * 2048:b * 8192 + (qq + 1) * 2048],
                in_=wkvd[n][:, qq * 2048:(qq + 1) * 2048])),
                waits=([allffn1_p, wkv_buf_free[b]] if qq == 0
                       else [T(sgwkv[b])]),
                inc=(sgwkv[b], 16))
            kv_load_q.append(T(sgwkv[b]))
        kv_load = kv_load_q[3]

        is_v = n >= 4
        if n == 4:
            p.add("tensor", lambda e: e.matmul(
                PA[:, 0:16], R_ONES128A, R_ZERO16,
                start=True, stop=False, skip_group_check=True),
                waits=[zero16_v, bias_g], inc=None)
        for mg in range(4):
            banks_mm = []
            for mi in range(4):
                mtile = mg * 4 + mi
                bank = PF[mi]
                for k in range(NK):
                    p.add("tensor", (lambda e, bank=bank, k=k, b=b, mtile=mtile:
                          e.matmul(bank[:, :],
                                   A1[:, k * D + mtile * 128:k * D + (mtile + 1) * 128],
                                   WKV[:, b * 8192 + k * 512:b * 8192 + (k + 1) * 512],
                                   start=(k == 0), stop=False)),
                        waits=[kv_load_q[k // 4]] +
                        ([lnt_done_s, pf_w(bank)] if k == 0 else []), inc=None)
                brow = (ROWS[64:65, 128 + (n % 4) * 512:128 + (n % 4) * 512 + 512]
                        if is_v else
                        ROWS[32:33, 128 + (n % 4) * 512:128 + (n % 4) * 512 + 512])
                orow = R_ONES128B if is_v else R_ONES128A
                p.add("tensor", (lambda e, bank=bank, brow=brow, orow=orow: e.matmul(
                    bank[:, :], orow, brow,
                    start=False, stop=True)),
                    waits=[bias_g], inc=(SP, 1))
                banks_mm.append((bank, mtile, P()))
            for bank, mtile, mm_p in banks_mm:
                if not is_v:
                    p.add("scalar", (lambda e, bank=bank, mtile=mtile, n=n: e.copy(
                        A2[:, mtile * D + n * 512:mtile * D + (n + 1) * 512],
                        bank[:, :])), waits=[mm_p, wfsp_d], inc=(SS, 1))
                    pf_free[id(bank)] = S()
                    if n == 1:
                        kevac1_s[mtile] = S()
                    if n == 3:
                        kevac3_s[mtile] = S()
                else:
                    pi = mtile % 2
                    p.add("vector", (lambda e, bank=bank, mtile=mtile, n=n, pi=pi:
                          e.tensor_tensor(
                              TBL[:, pi * 512:(pi + 1) * 512], bank[:, :],
                              A2[:, mtile * D + (n - 4) * 512:
                                  mtile * D + (n - 4) * 512 + 512], ALU.mult)),
                        waits=[mm_p, ptmp_free[pi]], inc=(SV, 1))
                    pf_free[id(bank)] = V()
                    pt_v = V()
                    for c in range(4):
                        cc = 4 * (n - 4) + c
                        p.add("tensor", (lambda e, c=c, cc=cc, mtile=mtile, pi=pi:
                              e.matmul(PA[:, cc:cc + 1],
                                       TBL[:, pi * 512 + c * 128:
                                           pi * 512 + (c + 1) * 128],
                                       STATB[:, CB_RSTDK + mtile:CB_RSTDK + mtile + 1],
                                       start=False, stop=(mtile == 15),
                                       skip_group_check=True)),
                            waits=[pt_v, rstdk_v[mtile]] if c == 0 else [],
                            inc=(SP, 1) if c == 3 else None)
                    ptmp_free[pi] = P()
        wkv_buf_free[b] = P()

        if n in (1, 2):
            for mtile in (range(8) if n == 1 else range(8, 16)):
                r = mtile % 2
                tb_c = r * 1024
                tb_s = r * 1024 + 512
                p.add("sync", (lambda e, mtile=mtile, tb_c=tb_c: e.dma_start(
                    out=TBL[:, tb_c:tb_c + 1024],
                    in_=ttokd[mtile * 128:(mtile + 1) * 128, :])),
                    waits=[tbl_free[r]], inc=(sgt[r], 16))
                tbl_d = T(sgt[r])
                E = A2[:, mtile * D:mtile * D + 512]
                O = A2[:, mtile * D + 512:mtile * D + 1024]
                p.add("vector", (lambda e, E=E, tb_c=tb_c: e.tensor_tensor(
                    SCRF[:, 0:512], E, TBL[:, tb_c:tb_c + 512], ALU.mult)),
                    waits=[tbl_d, kevac1_s[mtile]], inc=None)
                p.add("vector", (lambda e, E=E, tb_s=tb_s: e.tensor_tensor(
                    SCRF[:, 512:1024], E, TBL[:, tb_s:tb_s + 512], ALU.mult)),
                    inc=None)
                p.add("vector", (lambda e, E=E, O=O, tb_s=tb_s: e.tensor_tensor(
                    E, O, TBL[:, tb_s:tb_s + 512], ALU.mult)), inc=None)
                p.add("vector", (lambda e, E=E: e.tensor_tensor(
                    E, SCRF[:, 0:512], E, ALU.subtract)), inc=None)
                p.add("vector", (lambda e, O=O, tb_c=tb_c: e.tensor_tensor(
                    SCRF[:, 0:512], O, TBL[:, tb_c:tb_c + 512], ALU.mult)), inc=None)
                p.add("vector", (lambda e, O=O: e.tensor_tensor(
                    O, SCRF[:, 0:512], SCRF[:, 512:1024], ALU.add)), inc=(SV, 1))
                rope_done_v[mtile] = V()
                tbl_free[r] = rope_done_v[mtile]

        if n == 3:
            for mtile in range(NT):
                p.add("scalar", (lambda e, mtile=mtile: e.activation(
                    X32[:, 0:D], A2[:, mtile * D:(mtile + 1) * D], AF.Square,
                    accum_out=STAT[:, C_SS:C_SS + 1])),
                    waits=[rope_done_v[mtile], kevac3_s[mtile], x0_free[NT - 1],
                           rstdk_v.get(mtile - 1)],
                    inc=None)
                p.add("scalar", lambda e: e.activation(
                    STAT[:, C_SS:C_SS + 1], STAT[:, C_SS:C_SS + 1], AF.Ln,
                    bias=0.0, scale=1.0), inc=None)
                p.add("scalar", lambda e: e.activation(
                    STAT[:, C_SS:C_SS + 1], STAT[:, C_SS:C_SS + 1], AF.Exp,
                    bias=0.0, scale=-0.5), inc=(SS, 1))
                rk_s = S()
                p.add("vector", lambda e: e.tensor_scalar(
                    STAT[:, C_SS:C_SS + 1], STAT[:, C_SS:C_SS + 1], 1e12, None,
                    ALU.min), waits=[rk_s], inc=None)
                p.add("vector", (lambda e, mtile=mtile: e.tensor_copy(
                    STATB[:, CB_RSTDK + mtile:CB_RSTDK + mtile + 1],
                    STAT[:, C_SS:C_SS + 1])), inc=(SV, 1))
                rstdk_v[mtile] = V()
            ptmp_free[0] = tbl_free[0]
            ptmp_free[1] = tbl_free[0]

    if limit <= 2:
        return _finish()
    pa_done = P()
    tbl_free[0] = pa_done
    tbl_free[1] = pa_done
    p.add("vector", lambda e: e.tensor_copy(ASB[:], PA[:, 0:16]),
          waits=[pa_done], inc=(SV, 1))
    apart_v = V()
    p.add("sync", lambda e: e.dma_start(out=arin, in_=ASB[:]),
          waits=[apart_v], inc=(sar, 16))
    arin_d = T(sar)
    p.add("gpsimd", lambda e: e.collective_compute(
        "AllReduce", ALU.add, replica_groups=[[0, 1, 2, 3], [4, 5, 6, 7]],
        ins=[arin], outs=[arout]), waits=[arin_d], inc=(CC, 1))
    p.add("sync", lambda e: e.dma_start(out=ASB[:], in_=arout),
          waits=[(CC, 1), apart_v], inc=(sar, 16))
    asb_d = T(sar)
    if dbg:
        dbgA = nc.dram_tensor("dbgA", [128, 16], F32, kind="ExternalOutput").ap()
        p.add("sync", lambda e: e.dma_start(out=dbgA, in_=ASB[:]),
              waits=[asb_d], inc=(sar, 16))
    if limit <= 3:
        return _finish()

    a2_last_read = V()
    p.add("gpsimd", lambda e: e.dma_start(out=A2[:, :], in_=woutd),
        waits=[a2_last_read, wkv_buf_free[0], wkv_buf_free[1]], inc=(sgwo, 16))
    wout_g = T(sgwo)

    QWIN = 4096
    mq_order = [0, 4, 1, 5, 2, 6, 3, 7] + list(range(8, 16))
    wq_buf_free = {0: None, 1: None, 2: None}
    qwin_free = {0: None, 1: None, 2: None}
    qev_s = {}
    spill_d = {}
    last_spill = None

    def slot_of(mq):
        return 0 if mq < 4 else (1 if mq < 8 else 2)

    def wqb(b):
        return b * 2048 if b < 2 else 10240

    for idx, mq in enumerate(mq_order):
        b = idx % 3
        wq_load_h = []
        for hh in range(2):
            p.add("gpsimd", (lambda e, mq=mq, b=b, hh=hh: e.dma_start(
                out=WKV[:, wqb(b) + hh * 1024:wqb(b) + (hh + 1) * 1024],
                in_=wqd[mq][:, hh * 1024:(hh + 1) * 1024])),
                waits=([allffn1_p, wkv_buf_free[0], wkv_buf_free[1],
                        wq_buf_free[b]] if hh == 0 else [T(sgwq[b])]),
                inc=(sgwq[b], 16))
            wq_load_h.append(T(sgwq[b]))
        wq_load = wq_load_h[1]
        for k in range(NK):
            for sl in range(4):
                p.add("tensor", (lambda e, k=k, sl=sl, b=b: e.matmul(
                    PF[sl][:, :], WKV[:, wqb(b) + k * 128:wqb(b) + (k + 1) * 128],
                    A1[:, k * D + sl * 512:k * D + (sl + 1) * 512],
                    start=(k == 0), stop=(k == NK - 1))),
                    waits=[wq_load_h[k // 8]] + ([pf_w(PF[sl])] if k == 0 else []),
                    inc=(SP, 1) if (k == NK - 1 and sl == 3) else None)
        wq_buf_free[b] = P()
        qmm_p = P()
        w = slot_of(mq)
        for sl in range(4):
            p.add("scalar", (lambda e, sl=sl, w=w, mq=mq: e.activation(
                WKV[:, QWIN + w * 2048 + sl * 512:QWIN + w * 2048 + (sl + 1) * 512],
                PF[sl][:, :], AF.Identity, bias=BQ[:, mq:mq + 1], scale=1.0)),
                waits=[qmm_p, qwin_free[w]] if sl == 0 else [], inc=(SS, 1))
            pf_free[id(PF[sl])] = S()
        qev_s[mq] = S()

        if 4 <= mq < 8:
            c = mq - 4
            last_v = None
            for sl4 in range(4):
                r = sl4 % 2
                tb_c = r * 1024
                tb_s = r * 1024 + 512
                p.add("sync", (lambda e, c=c, sl4=sl4, tb_c=tb_c: e.dma_start(
                    out=TBL[:, tb_c:tb_c + 1024].rearrange("p (h o) -> p h o", h=2),
                    in_=tfeatd[c * 128:(c + 1) * 128, :].rearrange(
                        "p (h t) -> p h t", h=2)[:, :, sl4 * 512:(sl4 + 1) * 512])),
                    waits=[tbl_free[r]], inc=(sgt[r], 16))
                tbl_d = T(sgt[r])
                E = WKV[:, QWIN + sl4 * 512:QWIN + (sl4 + 1) * 512]
                O = WKV[:, QWIN + 2048 + sl4 * 512:QWIN + 2048 + (sl4 + 1) * 512]
                p.add("vector", (lambda e, E=E, tb_c=tb_c: e.tensor_tensor(
                    SCRF[:, 0:512], E, TBL[:, tb_c:tb_c + 512], ALU.mult)),
                    waits=[tbl_d, qev_s[c], qev_s[c + 4]], inc=None)
                p.add("vector", (lambda e, E=E, tb_s=tb_s: e.tensor_tensor(
                    SCRF[:, 512:1024], E, TBL[:, tb_s:tb_s + 512], ALU.mult)), inc=None)
                p.add("vector", (lambda e, E=E, O=O, tb_s=tb_s: e.tensor_tensor(
                    E, O, TBL[:, tb_s:tb_s + 512], ALU.mult)), inc=None)
                p.add("vector", (lambda e, E=E: e.tensor_tensor(
                    E, SCRF[:, 0:512], E, ALU.subtract)), inc=None)
                p.add("vector", (lambda e, O=O, tb_c=tb_c: e.tensor_tensor(
                    SCRF[:, 0:512], O, TBL[:, tb_c:tb_c + 512], ALU.mult)), inc=None)
                p.add("vector", (lambda e, O=O: e.tensor_tensor(
                    O, SCRF[:, 0:512], SCRF[:, 512:1024], ALU.add)), inc=(SV, 1))
                tbl_free[r] = V()
                last_v = V()
            for (w2, ch) in ((0, c), (1, c + 4)):
                p.add("vector", (lambda e, w2=w2, ch=ch: e.tensor_scalar(
                    WKV[:, QWIN + w2 * 2048:QWIN + (w2 + 1) * 2048],
                    WKV[:, QWIN + w2 * 2048:QWIN + (w2 + 1) * 2048],
                    ASB[:, ch:ch + 1], None, ALU.mult)),
                    waits=[asb_d, last_v], inc=(SV, 1))
                yv = V()
                for mg4 in range(4):
                    p.add("sync", (lambda e, w2=w2, ch=ch, mg4=mg4: e.dma_start(
                        out=ytsp[mg4 * 4:(mg4 + 1) * 4, :,
                                 ch * 128:(ch + 1) * 128].rearrange(
                            "m p t -> p m t"),
                        in_=WKV[:, QWIN + w2 * 2048 + mg4 * 512:
                                QWIN + w2 * 2048 + (mg4 + 1) * 512].rearrange(
                            "p (m t) -> p m t", m=4),
                        allow_slow_non_contiguous=True)),
                        waits=[yv] if mg4 == 0 else [T(sspy[w2])],
                        inc=(sspy[w2], 16))
                spill_d[ch] = T(sspy[w2])
                qwin_free[w2] = spill_d[ch]
                last_spill = spill_d[ch]
        elif mq >= 8:
            p.add("vector", (lambda e, mq=mq: e.tensor_scalar(
                WKV[:, QWIN + 2 * 2048:QWIN + 3 * 2048],
                WKV[:, QWIN + 2 * 2048:QWIN + 3 * 2048],
                ASB[:, mq:mq + 1], None, ALU.mult)),
                waits=[asb_d, qev_s[mq]], inc=(SV, 1))
            yv = V()
            for mg4 in range(4):
                p.add("sync", (lambda e, mq=mq, mg4=mg4: e.dma_start(
                    out=ytsp[mg4 * 4:(mg4 + 1) * 4, :,
                             mq * 128:(mq + 1) * 128].rearrange("m p t -> p m t"),
                    in_=WKV[:, QWIN + 2 * 2048 + mg4 * 512:
                            QWIN + 2 * 2048 + (mg4 + 1) * 512].rearrange(
                        "p (m t) -> p m t", m=4),
                    allow_slow_non_contiguous=True)),
                    waits=[yv] if mg4 == 0 else [T(sspy[2])],
                    inc=(sspy[2], 16))
            spill_d[mq] = T(sspy[2])
            qwin_free[2] = spill_d[mq]
            last_spill = spill_d[mq]

    q_done_p = P()
    if limit <= 4:
        return _finish()
    p.add("sync", lambda e: e.dma_start(out=A1[:, :], in_=wfsp),
          waits=[q_done_p], inc=(swsp, 16))
    wft_d = T(swsp)

    YT0 = 0
    X1R = 8192
    Q2T = 12288
    yt_free = {0: None, 1: None}
    x1r_free = {0: None, 1: None}
    tmpb_p2_free = q_done_p
    x32_p2_free = {0: None, 1: None}
    out_d = None
    ff16_free = {0: None, 1: None}
    stage_b_state = {}
    stage_a_state = {}

    load_state = {}

    def load_a(m):
        r = m % 2
        p.add("sync", (lambda e, m=m, r=r: e.dma_start(
            out=WKV[:, YT0 + r * 2048:YT0 + (r + 1) * 2048],
            in_=ytsp[m])),
            waits=([T(sspy[0]), T(sspy[1]), T(sspy[2]), yt_free[r],
                    wq_buf_free[0], wq_buf_free[1]]
                   if m < 2 else [yt_free[r]]),
            inc=(syt[r], 16))
        yt_d = T(syt[r])
        p.add("sync", (lambda e, m=m, r=r: e.dma_start(
            out=WKV[:, X1R + r * 2048:X1R + (r + 1) * 2048],
            in_=x1sp[m * 128:(m + 1) * 128, :])),
            waits=[T(sspy[0]), T(sspy[1]), T(sspy[2]), x1r_free[r],
                   wq_buf_free[2]]
            if m < 2 else [x1r_free[r]],
            inc=(sx1r[r], 16))
        x1_d = T(sx1r[r])
        load_state[m] = (yt_d, x1_d)

    def stage_a(m):
        nonlocal tmpb_p2_free, out_d
        r = m % 2
        yt_d, x1_d = load_state.pop(m)
        x2 = X32[:, r * D:(r + 1) * D]
        p.add("scalar", (lambda e, m=m, r=r, x2=x2: e.activation(
            x2, WKV[:, X1R + r * 2048:X1R + (r + 1) * 2048], AF.Identity,
            bias=STAT[:, C_NM2 + m:C_NM2 + m + 1],
            scale=STAT[:, C_R2P1 + m:C_R2P1 + m + 1])),
            waits=[x1_d, x32_p2_free[r]], inc=(SS, 1))
        x2s_s = S()
        defer = []
        for n in range(4):
            bank = PF[n % 2]
            for k in range(NK):
                p.add("tensor", (lambda e, bank=bank, k=k, n=n, r=r: e.matmul(
                    bank[:, :],
                    WKV[:, YT0 + r * 2048 + k * 128:YT0 + r * 2048 + (k + 1) * 128],
                    A2[:, k * D + n * 512:k * D + (n + 1) * 512],
                    start=(k == 0), stop=False)),
                    waits=[yt_d, wout_g, pf_w(bank)] if k == 0 else [], inc=None)
            p.add("tensor", (lambda e, bank=bank, n=n: e.matmul(
                bank[:, :], R_ONES128A, R_BO[0:1, n * 512:(n + 1) * 512],
                start=False, stop=True)), inc=(SP, 1))
            wmm_p = P()
            if n < 2:
                p.add("vector", (lambda e, bank=bank, n=n, x2=x2: e.tensor_tensor(
                    x2[:, n * 512:(n + 1) * 512], x2[:, n * 512:(n + 1) * 512],
                    bank[:, :], ALU.add)),
                    waits=[wmm_p, x2s_s], inc=(SV, 1))
                pf_free[id(bank)] = V()
            else:
                defer.append((bank, wmm_p, n))
        yt_free[r] = P()
        stage_a_state[m] = (r, x2, x2s_s, defer)

    def stage_a2(m):
        r, x2, x2s_s, defer = stage_a_state.pop(m)
        for bank, wmm_p, n in defer:
            p.add("vector", (lambda e, bank=bank, n=n, x2=x2: e.tensor_tensor(
                x2[:, n * 512:(n + 1) * 512], x2[:, n * 512:(n + 1) * 512],
                bank[:, :], ALU.add)),
                waits=[wmm_p, x2s_s], inc=(SV, 1))
            pf_free[id(bank)] = V()
        x2_v = V()
        x1r_free[r] = x2_v
        stage_b_state[m] = (r, x2, x2_v)

    def stage_b1(m):
        r, x2, x2_v = stage_b_state[m]
        for c in range(4):
            p.add("vector", (lambda e, c=c, x2=x2: e.bn_stats(
                BNS[:, c * 6:(c + 1) * 6], x2[:, c * 512:(c + 1) * 512])), inc=None)
        p.add("vector", lambda e: e.bn_aggr(BNS[:, 24:26], BNS[:, 0:24]), inc=None)
        p.add("vector", lambda e: e.tensor_scalar(
            STAT[:, C_RSTD:C_RSTD + 1], BNS[:, 25:26], LN_EPS, None, ALU.add),
            inc=None)
        p.add("vector", (lambda e, x2=x2: e.tensor_reduce(
            STAT[:, C_TA:C_TA + 1], x2, AX, ALU.max)), inc=None)
        p.add("vector", (lambda e, x2=x2: e.tensor_reduce(
            STAT[:, C_TB:C_TB + 1], x2, AX, ALU.min)), inc=(SV, 1))
        st3_v = V()
        p.add("scalar", lambda e: e.activation(
            STAT[:, C_RSTD:C_RSTD + 1], STAT[:, C_RSTD:C_RSTD + 1], AF.Ln,
            bias=0.0, scale=1.0), waits=[st3_v], inc=None)
        p.add("scalar", lambda e: e.activation(
            STAT[:, C_RSTD:C_RSTD + 1], STAT[:, C_RSTD:C_RSTD + 1], AF.Exp,
            bias=0.0, scale=-0.5), inc=(SS, 1))
        r3_s = S()
        p.add("vector", (lambda e, r=r: e.tensor_copy(
            STAT[:, C_M3 + r:C_M3 + r + 1], BNS[:, 24:25])), waits=[r3_s], inc=None)
        p.add("vector", lambda e: e.tensor_tensor(
            STAT[:, C_TA:C_TA + 1], STAT[:, C_TA:C_TA + 1], BNS[:, 24:25],
            ALU.subtract), inc=None)
        p.add("vector", lambda e: e.tensor_tensor(
            STAT[:, C_TB:C_TB + 1], BNS[:, 24:25], STAT[:, C_TB:C_TB + 1],
            ALU.subtract), inc=None)
        p.add("vector", lambda e: e.tensor_tensor(
            STAT[:, C_TA:C_TA + 1], STAT[:, C_TA:C_TA + 1], STAT[:, C_TB:C_TB + 1],
            ALU.max), inc=None)
        p.add("vector", lambda e: e.tensor_tensor(
            STAT[:, C_TA:C_TA + 1], STAT[:, C_TA:C_TA + 1], STAT[:, C_RSTD:C_RSTD + 1],
            ALU.mult), inc=None)
        p.add("vector", lambda e: e.tensor_scalar(
            STAT[:, C_TA:C_TA + 1], STAT[:, C_TA:C_TA + 1], Q_EPS, None, ALU.max),
            inc=None)
        p.add("vector", (lambda e, r=r: e.tensor_scalar(
            STAT[:, C_ISH3 + r:C_ISH3 + r + 1], STAT[:, C_TA:C_TA + 1],
            0.5 / 127.0, None, ALU.mult)), inc=None)
        p.add("vector", lambda e: e.reciprocal(
            STAT[:, C_TB:C_TB + 1], STAT[:, C_TA:C_TA + 1]), inc=None)
        p.add("vector", lambda e: e.tensor_scalar(
            STAT[:, C_TB:C_TB + 1], STAT[:, C_TB:C_TB + 1], 127.0, None, ALU.mult),
            inc=None)
        p.add("vector", (lambda e, r=r: e.tensor_tensor(
            STAT[:, C_G3 + r:C_G3 + r + 1], STAT[:, C_TB:C_TB + 1],
            STAT[:, C_RSTD:C_RSTD + 1], ALU.mult)), inc=None)
        p.add("vector", (lambda e, r=r: e.tensor_copy(
            STATB[:, CB_H2 + r:CB_H2 + r + 1], STAT[:, C_TB:C_TB + 1])), inc=(SV, 1))
        h2c_v = V()
        p.add("vector", (lambda e, x2=x2, r=r: e.tensor_scalar(
            QI8[:], x2, STAT[:, C_M3 + r:C_M3 + r + 1],
            STAT[:, C_G3 + r:C_G3 + r + 1], ALU.subtract, ALU.mult)), inc=None)
        p.add("vector", lambda e: e.tensor_copy(TBL[:, 0:D], QI8[:]),
              waits=[tmpb_p2_free], inc=(SV, 1))
        q3_v = V()
        stage_b_state[m] = (r, x2, h2c_v, q3_v)

    def stage_b2(m):
        nonlocal tmpb_p2_free, out_d
        r, x2, h2c_v, q3_v = stage_b_state.pop(m)
        ptb = PT[(m + 1) % 2]
        p.add("tensor", (lambda e, ptb=ptb, r=r: e.transpose(
            ptb[0:1, 0:128], STATB[:, CB_H2 + r:CB_H2 + r + 1], IDENT[:])),
            waits=[pt_w(ptb), h2c_v], inc=(SP, 1))
        p.add("scalar", (lambda e, ptb=ptb, m=m: e.copy(R_H2(m), ptb[0:1, 0:128])),
              waits=[P()], inc=(SS, 1))
        pt_free[id(ptb)] = S()
        h2r_s = S()

        evac_s = []
        for half in range(2):
            ptb = PT[half]
            for j in range(8):
                k = half * 8 + j
                p.add("tensor", (lambda e, ptb=ptb, j=j, k=k: e.transpose(
                    ptb[:, j * 128:(j + 1) * 128], TBL[:, k * 128:(k + 1) * 128],
                    IDENT[:])),
                    waits=[pt_w(ptb), q3_v] if j == 0 else [],
                    inc=(SP, 1) if j == 7 else None)
            tr_p = P()
            for j in range(8):
                k = half * 8 + j
                p.add("scalar", (lambda e, ptb=ptb, j=j, k=k: e.copy(
                    WKV[:, Q2T + k * 128:Q2T + (k + 1) * 128],
                    ptb[:, j * 128:(j + 1) * 128])),
                    waits=[tr_p] if j == 0 else [], inc=(SS, 1) if j == 7 else None)
                evac_s.append(S())
            pt_free[id(ptb)] = S()
        tmpb_p2_free = P()

        x3_vs = {}
        for n in range(4):
            bank = PF[2 + n % 2]
            for k in range(NK):
                p.add("tensor", (lambda e, bank=bank, k=k, n=n: e.matmul(
                    bank[:, :], WKV[:, Q2T + k * 128:Q2T + (k + 1) * 128],
                    A1[:, k * D + n * 512:k * D + (n + 1) * 512],
                    start=(k == 0), stop=False)),
                    waits=[evac_s[k], wft_d] + ([pf_w(bank)] if k == 0 else []),
                    inc=None)
            p.add("tensor", (lambda e, bank=bank, n=n, m=m: e.matmul(
                bank[:, :], R_H2(m), R_FB[0:1, n * 512:(n + 1) * 512],
                start=False, stop=True)),
                waits=[h2r_s], inc=(SP, 1))
            f2_p = P()
            sl = n % 2
            p.add("scalar", (lambda e, bank=bank, sl=sl, r=r: e.activation(
                FF16[:, sl * 512:(sl + 1) * 512], bank[:, :], AF.Identity,
                bias=0.0, scale=STAT[:, C_ISH3 + r:C_ISH3 + r + 1])),
                waits=[f2_p, ff16_free[sl]], inc=(SS, 1))
            pf_free[id(bank)] = S()
            ff2_s = S()
            p.add("vector", (lambda e, sl=sl, n=n, x2=x2: e.tensor_tensor(
                x2[:, n * 512:(n + 1) * 512], x2[:, n * 512:(n + 1) * 512],
                FF16[:, sl * 512:(sl + 1) * 512], ALU.add)),
                waits=[ff2_s], inc=(SV, 1))
            ff16_free[sl] = V()
            x3_vs[n] = V()
        p.add("sync", (lambda e, m=m, x2=x2: e.dma_start(
            out=outd[m * 128:(m + 1) * 128, :], in_=x2)),
            waits=[x3_vs[3]], inc=(sout[r], 16))
        out_d = T(sout[r])
        x32_p2_free[r] = out_d

    load_a(0)
    for m in range(p2_tiles):
        if m + 1 < p2_tiles:
            load_a(m + 1)
        stage_a(m)
        if m >= 1:
            stage_b1(m - 1)
        stage_a2(m)
        if m >= 1:
            stage_b2(m - 1)
    stage_b1(p2_tiles - 1)
    stage_b2(p2_tiles - 1)

    p.add("sync", lambda e: e.nop(), waits=[T(sout[0]), T(sout[1])])
    return _finish()


_theta = 10000.0


def _tables(core):
    pos = (core % 4) * TPC + np.arange(TPC, dtype=np.float64)
    j = np.arange(NPAIR, dtype=np.float64)
    inv = _theta ** (-2.0 * j / 1024.0)
    ang = pos[:, None] * inv[None, :]
    return np.cos(ang).astype(np.float32), np.sin(ang).astype(np.float32)


def kernel(L0_attn_X, L0_attn_W_in, L0_attn_b_in, L0_attn_W_out, L0_attn_b_out,
           L0_ffn_weight, L0_ffn_w_scale, L0_ffn_bias):
    X = np.asarray(L0_attn_X)
    W_in = np.asarray(L0_attn_W_in)
    b_in = np.asarray(L0_attn_b_in)
    W_out = np.asarray(L0_attn_W_out)
    b_out = np.asarray(L0_attn_b_out)
    Wf = np.asarray(L0_ffn_weight)
    ws = np.asarray(L0_ffn_w_scale)
    fb = np.asarray(L0_ffn_bias)

    wq0 = W_in[0:D][PERM, :].T
    wq = np.ascontiguousarray(
        wq0.reshape(16, 128, 16, 128).transpose(2, 1, 0, 3).reshape(16, 128, D))
    wkv0 = np.concatenate([W_in[D:2 * D][PERM, :], W_in[2 * D:3 * D][PERM, :]],
                          axis=0).T
    wkv = np.ascontiguousarray(
        wkv0.reshape(16, 128, 8, 512).transpose(2, 1, 0, 3).reshape(8, 128, 8192))
    wout0 = W_out[:, PERM].T
    wout = np.ascontiguousarray(
        wout0.reshape(16, 128, D).transpose(1, 0, 2).reshape(128, NK * D))
    wf = np.ascontiguousarray(
        Wf.T.reshape(16, 128, D).transpose(1, 0, 2).reshape(128, NK * D))
    wsr = np.ascontiguousarray(ws.reshape(1, D))
    bq = np.ascontiguousarray(b_in[0:D][PERM])
    bk = np.ascontiguousarray(b_in[D:2 * D][PERM].reshape(1, D))
    bv = np.ascontiguousarray(b_in[2 * D:3 * D][PERM].reshape(1, D))
    bo = np.ascontiguousarray(b_out.reshape(1, D))
    fbr = np.ascontiguousarray(fb.reshape(1, D))
    ident = np.eye(128, dtype=BF)
    ones32 = np.ones((1, 128), dtype=np.float32)

    nc = build_program()
    in_maps = []
    Xf = np.ascontiguousarray(X.reshape(8, TPC, D))
    for c in range(8):
        ct, st = _tables(c)
        in_maps.append({
            "x0": Xf[c], "wq": wq, "wkv": wkv, "wout": wout, "wf": wf,
            "ws": wsr, "bq": bq, "bk": bk, "bv": bv, "bo": bo, "fb": fbr,
            "ttok": np.ascontiguousarray(
                np.concatenate([ct, st], axis=1).astype(BF)),
            "tfeat": np.ascontiguousarray(
                np.concatenate([ct.T, st.T], axis=1).astype(BF)),
            "ident": ident, "ones32": ones32,
        })
    res = run_bass_kernel_spmd(nc, in_maps, core_ids=list(range(8)))
    out = np.stack([res.results[c]["out"] for c in range(8)], axis=0)
    return np.ascontiguousarray(out.reshape(2, 8192, D))
